# version 32
# baseline (speedup 1.0000x reference)
import numpy as np
import ml_dtypes
import concourse.bass as bass
import concourse.mybir as mybir
from concourse.bass import IndirectOffsetOnAxis
from concourse.bass_utils import run_bass_kernel_spmd

F32 = mybir.dt.float32
BF16 = mybir.dt.bfloat16
U8 = mybir.dt.uint8
I32 = mybir.dt.int32
AF = mybir.ActivationFunctionType
ALU = mybir.AluOpType

NEG = -1.0e9
EXPB = -20.0
EPS = 1.0e-5


class Prog:
    ENGINES = ("te", "ve", "se", "gp", "sy")

    class Op:
        __slots__ = ("idx", "eng", "fn", "kind", "dma_sem", "dma_idx",
                     "needs_inc", "cnt", "deps", "dep_dma")

    def __init__(self):
        self.ops = []
        self.dma_count = {}
        self.last_writer = {}
        self.readers = {}
        self.bar_ops = []
        self.bar_dma = {}

    def _record(self, eng, reads, writes, fn, kind, dma_sem=None):
        op = Prog.Op()
        op.idx = len(self.ops)
        op.eng = eng
        op.fn = fn
        op.kind = kind
        op.dma_sem = dma_sem
        op.needs_inc = False
        op.cnt = None
        op.deps = []
        op.dep_dma = {}

        deps = []
        for k in reads:
            w = self.last_writer.get(k)
            if w is not None:
                deps.append(w)
        for k in writes:
            w = self.last_writer.get(k)
            if w is not None:
                deps.append(w)
            deps.extend(self.readers.get(k, ()))
        deps.extend(self.bar_ops)

        seen = set()
        for d in deps:
            if d is op or id(d) in seen:
                continue
            seen.add(id(d))
            if d.kind == "d":
                want = 16 * d.dma_idx
                if op.dep_dma.get(d.dma_sem, 0) < want:
                    op.dep_dma[d.dma_sem] = want
            else:
                if d.eng == "te" and eng == "te":
                    continue
                d.needs_inc = True
                op.deps.append(d)
        for s, c in self.bar_dma.items():
            want = 16 * c
            if op.dep_dma.get(s, 0) < want:
                op.dep_dma[s] = want
        if kind == "d":
            prev = self.dma_count.get(dma_sem, 0)
            if prev > 0 and op.dep_dma.get(dma_sem, 0) < 16 * prev:
                op.dep_dma[dma_sem] = 16 * prev
            self.dma_count[dma_sem] = prev + 1
            op.dma_idx = prev + 1
        else:
            op.dma_idx = None

        for k in writes:
            self.last_writer[k] = op
            self.readers[k] = []
        for k in reads:
            self.readers.setdefault(k, []).append(op)
        self.ops.append(op)
        return op

    def c(self, eng, reads, writes, fn):
        return self._record(eng, reads, writes, fn, "c")

    def dma(self, eng, reads, writes, fn, sem):
        return self._record(eng, reads, writes, fn, "d", dma_sem=sem)

    def barrier(self):
        last = {}
        for op in self.ops:
            if op.kind == "c":
                last[op.eng] = op
        self.bar_ops = list(last.values())
        for op in self.bar_ops:
            op.needs_inc = True
        self.bar_dma = dict(self.dma_count)
        self.last_writer = {}
        self.readers = {}

    def emit(self, nc, block, es):
        cnt = {e: 0 for e in Prog.ENGINES}
        for op in self.ops:
            if op.kind == "c" and op.needs_inc:
                cnt[op.eng] += 1
                op.cnt = cnt[op.eng]
        sems = {e: es.enter_context(nc.semaphore(f"sem_{e}"))
                for e in Prog.ENGINES if cnt[e] > 0}
        dsems = {s: es.enter_context(nc.semaphore(f"dma_{s}"))
                 for s in self.dma_count}
        streams = {e: [op for op in self.ops if op.eng == e]
                   for e in Prog.ENGINES}

        def make_body(engname):
            ops = streams[engname]

            def body(eng):
                water = {}
                for op in ops:
                    for d in op.deps:
                        if water.get(d.eng, 0) < d.cnt:
                            water[d.eng] = d.cnt
                            eng.wait_ge(sems[d.eng], d.cnt)
                    for s, val in op.dep_dma.items():
                        if water.get("D" + s, 0) < val:
                            water["D" + s] = val
                            eng.wait_ge(dsems[s], val)
                    inst = op.fn(eng)
                    if op.kind == "d":
                        inst.then_inc(dsems[op.dma_sem], 16)
                    elif op.needs_inc:
                        inst.then_inc(sems[op.eng], 1)
            return body

        names = {"te": "tensor", "ve": "vector", "se": "scalar",
                 "gp": "gpsimd", "sy": "sync"}
        for e in Prog.ENGINES:
            if streams[e]:
                getattr(block, names[e])(make_body(e))


def default_cfg():
    return dict(T=2048, D=1024, F=4096, H=16, R=4)


def build_program(cfg):
    T, D, F, H, R = cfg["T"], cfg["D"], cfg["F"], cfg["H"], cfg["R"]
    HD = 64
    assert D == H * HD
    NT = T // 128
    NC = D // 128
    NF = F // 128
    NQC = T // 512
    QCW = 512
    KSEL = T // 2
    NJ = KSEL // 128
    NHP = H // 2
    ET = NC
    DKT = 4
    NDH = D // 512
    KH = KSEL // 512
    VW = H * (HD + 1)

    nc = bass.Bass("TRN2", target_bir_lowering=False)

    x_d = nc.dram_tensor("x", [R * T, D], F32, kind="ExternalInput")
    wqkv_d = nc.dram_tensor("w_qkv", [D, 3 * D], F32, kind="ExternalInput")
    wproj_d = nc.dram_tensor("w_proj", [D, D], F32, kind="ExternalInput")
    wr_d = nc.dram_tensor("w_router", [1, D], F32, kind="ExternalInput")
    w1_d = nc.dram_tensor("W1", [D, F], F32, kind="ExternalInput")
    b1_d = nc.dram_tensor("b1", [1, F], F32, kind="ExternalInput")
    w2_d = nc.dram_tensor("W2", [F, D], F32, kind="ExternalInput")
    b2_d = nc.dram_tensor("b2", [1, D], F32, kind="ExternalInput")
    y_d = nc.dram_tensor("y", [R * T, D], F32, kind="ExternalOutput")

    wqk_t = nc.dram_tensor("wqk_t", [2 * ET, NC, 128, 128], BF16, kind="Internal")
    wv_r = nc.dram_tensor("wv_r", [D, D], BF16, kind="Internal")
    wp_h = nc.dram_tensor("wp_h", [H, HD, D], BF16, kind="Internal")
    w1_t = nc.dram_tensor("w1_t", [NF, NC, 128, 128], BF16, kind="Internal")
    w2_r = nc.dram_tensor("w2_r", [F, D], BF16, kind="Internal")
    k_dram = nc.dram_tensor("k_dram", [ET, 128, T], BF16, kind="Internal")
    qb_d = nc.dram_tensor("qb_d", [NQC, 128, ET * 512], BF16, kind="Internal")
    x2_d = nc.dram_tensor("x2_d", [T, D], F32, kind="Internal")
    sr_d = nc.dram_tensor("sr_d", [1, T], F32, kind="Internal")
    tof_d = nc.dram_tensor("tof_d", [KSEL, 1], F32, kind="Internal")
    upd_d = nc.dram_tensor("upd_d", [T, D], BF16, kind="Internal")

    p = Prog()
    from contextlib import ExitStack
    es = ExitStack()
    with es:
        def sb(name, shape, dt):
            return es.enter_context(nc.sbuf_tensor(name, shape, dt)).ap()

        iota = sb("iota", [128, max(1024, KSEL)], F32)
        iotac = sb("iotac", [128, 1], F32)
        ident_f = sb("ident_f", [128, 128], F32)
        ident_b = sb("ident_b", [128, 128], BF16)
        masks = sb("masks", [128, (DKT - 1) * 128 + QCW], BF16)
        onesf = sb("onesf", [1, 128], F32)
        onesb = sb("onesb", [1, 128], BF16)
        cpool = sb("cpool", [128, 4], F32)
        wr_rep = sb("wr_rep", [128, D], F32)
        b1c = sb("b1c", [128, NF], F32)
        b2row = sb("b2row", [1, D], BF16)
        stat6 = [sb(f"stat6{i}", [128, 24], F32) for i in range(2)]
        stat2 = [sb(f"stat2{i}", [128, 2], F32) for i in range(2)]
        mcol = [sb(f"mcol{i}", [128, 1], F32) for i in range(2)]
        rcol = [sb(f"rcol{i}", [128, 1], F32) for i in range(2)]
        scol = sb("scol", [128, NT], F32)
        rankc = sb("rankc", [128, NT], F32)
        sslot = sb("sslot", [128, NT], F32)
        wcol = sb("wcol", [128, NT], F32)
        tmpc = sb("tmpc", [128, NT], F32)
        st16 = sb("st16", [NT, 128], F32)
        repb = sb("repb", [64, 1024], BF16)
        recbb = sb("recbb", [1, 1024], BF16)
        iotacb = sb("iotacb", [128, 1], BF16)
        toffb = sb("toffb", [128, NT], BF16)
        tofc = sb("tofc", [128, NJ], I32)
        tofcf = sb("tofcf", [128, NJ], F32)
        xin = [sb(f"xin{i}", [128, D], F32) for i in range(2)]
        x2in = [sb(f"x2in{i}", [128, D], F32) for i in range(2)]
        tmpf = [sb(f"tmpfa{i}", [128, D], F32) for i in range(2)]
        tmpf2 = [sb(f"tmpfb{i}", [128, D], F32) for i in range(2)]
        hp_b = [sb(f"hpb{i}", [128, D], BF16) for i in range(2)]
        wst = [sb(f"wst{i}", [128, 2048], BF16) for i in range(2)]
        kqb = [sb(f"kqb{i}", [128, 2048], BF16) for i in range(2)]
        qst = sb("qst", [128, ET * 512], BF16)
        h2 = sb("h2", [128, NT * D], BF16)

        w8 = sb("w8", [128, 8192], U8)
        attb = [w8[:, i * 2048:(i + 1) * 2048].bitcast(BF16) for i in range(4)]

        ptb = [w8[:, i * 2048:(i + 1) * 2048].bitcast(BF16)[:, 0:KSEL]
               for i in range(2)]
        pjb = [w8[:, 4096 + i * 2048:4096 + (i + 1) * 2048].bitcast(BF16)[:, 0:NJ * 128]
               for i in range(2)]
        oute = [w8[:, i * 4096:(i + 1) * 4096].bitcast(F32) for i in range(2)]
        W8A = ["w8a0", "w8a1", "w8b0", "w8b1", "w8g0", "w8g1"]

        GSZ_cast = 73728 + D * 4
        GSZ_B = NC * T * 2 + NT * VW * 2 + NC * D * 2
        GSZ_C = NC * T * 2 + NT * VW * 2 + H * QCW * 2 + 8192
        GSZ_D = T * 4 + NC * KSEL * 2 + max(NF * KSEL * 2, NT * D * 4)
        GSZ = max(GSZ_cast, GSZ_B, GSZ_C, GSZ_D)
        g = sb("g", [128, GSZ], U8)

        def gv(off, part, width, dt):
            bpe = 4 if dt == F32 else 2
            ap = g[:part, off:off + width * bpe].bitcast(dt)
            return ap
        wfp = [gv(i * 16384, 128, 4096, F32) for i in range(3)]
        wbp = [gv(49152 + i * 8192, 128, 4096, BF16) for i in range(3)]
        zbuf = gv(73728, 128, D, F32)
        w_f32 = wfp[0]
        w_bf = wbp[0]
        h1T = gv(0, 128, NC * T, BF16)
        vsb = gv(NC * T * 2, 128, NT * VW, BF16)
        _voff = NC * T * 2 + NT * VW * 2
        wpj = gv(0, 128, H // 2 * D, BF16)
        qst2 = gv(H // 2 * D * 2, 128, ET * 512, BF16)
        recb = gv(H // 2 * D * 2 + ET * 512 * 2, 1, 1024, F32)
        _goff = H // 2 * D * 2 + ET * 512 * 2 + 4096
        attg = [gv(_goff + i * 2048, 128, 1024, BF16) for i in range(2)]
        assert _goff + 4096 <= NC * T * 2
        attb = attb + attg
        ao = gv(_voff, 128, H // 2 * QCW, BF16)
        aostg = gv(_voff + H * QCW, 64, H // 2 * QCW, BF16)
        kst = [gv(_voff + H * QCW * 2 + i * 4096, 128, 2048, BF16)
               for i in range(2)]
        wv_all = gv(_voff, 128, NC * D, BF16)
        srow = gv(_voff + H * QCW * 2, 1, T, F32)
        tofr = gv(_voff + H * QCW * 2, 1, KSEL, F32)
        srep = gv(0, 128, T, F32)
        selT = gv(T * 4, 128, NC * KSEL, BF16)
        _zoff = T * 4 + NC * KSEL * 2
        zgT = gv(_zoff, 128, NF * KSEL, BF16)
        y2f = gv(_zoff, 128, NT * D, F32)
        usb = gv(T * 4, 128, NJ * D, BF16)
        assert _zoff + max(NF * KSEL * 2, NT * D * 4) <= GSZ
        assert T * 4 + NJ * D * 2 <= _zoff
        assert T * 4 <= _voff + H * QCW * 2

        PS = [es.enter_context(nc.psum_tensor(f"P{i}", [128, 1024], F32)).ap()
              for i in range(4)]
        PA, PB, PC, PD = PS
        PDb = PD.bitcast(BF16)
        PAb = PA.bitcast(BF16)
        PK = ["PA", "PB", "PC", "PD"]

        NST = (D + 511) // 512

        def ln_stats(srckeys, srcap, sl):
            s6, s2 = stat6[sl], stat2[sl]
            k6, k2, km, kr = f"stat6{sl}", f"stat2{sl}", f"mcol{sl}", f"rcol{sl}"
            for c in range(NST):
                p.c("ve", list(srckeys) + ([k6] if c else []), [k6],
                    lambda e, c=c: e.bn_stats(
                        out=s6[:, c * 6:(c + 1) * 6],
                        in_=srcap[:, c * 512:(c + 1) * 512]))
            p.c("ve", [k6], [k2], lambda e: e.bn_aggr(
                out=s2, in_=s6[:, 0:NST * 6]))
            p.c("ve", [k2], [km], lambda e: e.tensor_copy(
                out=mcol[sl], in_=s2[:, 0:1]))
            p.c("se", [k2, "cpool"], [kr], lambda e: e.activation(
                out=rcol[sl], in_=s2[:, 1:2], func=AF.Ln, bias=cpool[:, 0:1]))
            p.c("se", [kr], [kr], lambda e: e.activation(
                out=rcol[sl], in_=rcol[sl], func=AF.Exp, scale=-0.5))

        p.c("gp", [], ["iota"], lambda e: e.iota(
            iota, pattern=[[1, iota.shape[1]]], base=0, channel_multiplier=0,
            allow_small_or_imprecise_dtypes=True))
        p.c("gp", [], ["iotac"], lambda e: e.iota(
            iotac, pattern=[[0, 1]], base=0, channel_multiplier=1,
            allow_small_or_imprecise_dtypes=True))
        p.c("ve", ["iotac"], ["iotacb"], lambda e: e.tensor_copy(
            out=iotacb, in_=iotac))
        p.c("gp", [], ["toffb"], lambda e: e.iota(
            toffb, pattern=[[128, NT]], base=0, channel_multiplier=0,
            allow_small_or_imprecise_dtypes=True))
        p.c("ve", [], ["zb"], lambda e: e.memset(zbuf, 0.0))
        for i in range(NT):
            p.dma("gp", ["zb"], ["updd"], lambda e, i=i: e.dma_start(
                out=upd_d.ap()[i * 128:(i + 1) * 128, :],
                in_=zbuf), "dUZ")
        p.c("ve", [], ["cpool"], lambda e: e.memset(cpool[:, 0:1], EPS))
        p.c("ve", [], ["cpool2"], lambda e: e.memset(cpool[:, 1:2], EXPB))
        p.c("ve", [], ["onesf"], lambda e: e.memset(onesf, 1.0))
        p.c("ve", [], ["onesb"], lambda e: e.memset(onesb, 1.0))
        p.c("ve", ["iota", "iotac"], ["ident_f"], lambda e: e.tensor_scalar(
            out=ident_f, in0=iota[:, 0:128], scalar1=iotac, scalar2=None,
            op0=ALU.is_equal))
        p.c("ve", ["ident_f"], ["ident_b"], lambda e: e.tensor_copy(
            out=ident_b, in_=ident_f))
        MW = (DKT - 1) * 128
        p.c("ve", ["iotac"], ["tmpc"], lambda e: e.tensor_scalar(
            out=tmpc[:, 0:1], in0=iotac, scalar1=float(MW),
            scalar2=None, op0=ALU.add))
        p.c("ve", ["iota", "tmpc"], ["masks"], lambda e: e.tensor_scalar(
            out=masks, in0=iota[:, 0:MW + QCW],
            scalar1=tmpc[:, 0:1], scalar2=None, op0=ALU.is_ge))

        p.dma("sy", [], ["wf0"], lambda e: e.dma_start(
            out=w_f32[0:1, 0:D], in_=wr_d.ap()), "dW")
        for h in range(D // 512):
            pk = PK[h % 2]
            p.c("te", ["wf0", "onesf"], [pk], lambda e, h=h: e.matmul(
                out=PS[h % 2][:, 0:512], lhsT=onesf,
                rhs=w_f32[0:1, h * 512:(h + 1) * 512], start=True, stop=True))
            p.c("ve", [pk], ["wr_rep"], lambda e, h=h: e.tensor_copy(
                out=wr_rep[:, h * 512:(h + 1) * 512], in_=PS[h % 2][:, 0:512]))
        p.dma("sy", ["wr_rep"], ["wf0"], lambda e: e.dma_start(
            out=w_f32[0:1, 0:D], in_=b2_d.ap()), "dW")
        p.c("ve", ["wf0"], ["b2row"], lambda e: e.tensor_copy(
            out=b2row, in_=w_f32[0:1, 0:D]))
        p.c("ve", ["b2row"], ["wf0"], lambda e: e.memset(w_f32, 0.0))
        p.dma("sy", ["wf0"], ["wf0"], lambda e: e.dma_start(
            out=w_f32[0:1, 0:F], in_=b1_d.ap()), "dW")
        for ft in range(NF):
            pk = PK[2 + ft % 2]
            pp = PS[2 + ft % 2]
            p.c("te", ["wf0", "ident_f", pk], [pk],
                lambda e, ft=ft, pp=pp: e.transpose(
                    out=pp[:, 0:128], in_=w_f32[:, ft * 128:(ft + 1) * 128],
                    identity=ident_f))
            p.c("ve", [pk], ["b1c"], lambda e, ft=ft, pp=pp: e.tensor_copy(
                out=b1c[:, ft:ft + 1], in_=pp[:, 0:1]))

        _slot = [0]

        def prep_iter(load_fn, width, store_fns):
            s = _slot[0] % 3
            _slot[0] += 1
            wf, wfk = wfp[s], f"wf{s}"
            wb, wbk = wbp[s], f"wb{s}"
            p.dma("sy", [wfk], [wfk],
                  lambda e, load_fn=load_fn, wf=wf: load_fn(e, wf),
                  ["dW", "dWb", "dWc"][s])
            p.c("ve", [wfk], [wbk], lambda e, wb=wb, wf=wf, width=width:
                e.tensor_copy(out=wb[:, 0:width], in_=wf[:, 0:width]))
            for sf, dk in store_fns:
                p.dma("gp", [wbk], [dk],
                      lambda e, sf=sf, wb=wb: sf(e, wb),
                      ["dS", "dSb", "dSc"][s])

        for ci in range(NC):
            sts = []
            for s in range(2):
                sts.append((lambda e, wb, ci=ci, s=s: e.dma_start(
                    out=wqk_t.ap()[s * ET:(s + 1) * ET, ci, :, :].rearrange(
                        "t p j -> p t j"),
                    in_=wb[:, s * ET * 128:(s + 1) * ET * 128].rearrange(
                        "p (t j) -> p t j", j=128)), "wqkT"))
            sts.append((lambda e, wb, ci=ci: e.dma_start(
                out=wv_r.ap()[ci * 128:(ci + 1) * 128, :],
                in_=wb[:, 2 * D:3 * D]), "wvr"))
            prep_iter(lambda e, wf, ci=ci: e.dma_start(
                out=wf[:, 0:3 * D],
                in_=wqkv_d.ap()[ci * 128:(ci + 1) * 128, :]), 3 * D, sts)
        for ci in range(NC):
            sts = [(lambda e, wb, ci=ci, hh=hh: e.dma_start(
                out=wp_h.ap()[2 * ci + hh, :, :],
                in_=wb[hh * 64:(hh + 1) * 64, 0:D]), "wph") for hh in range(2)]
            prep_iter(lambda e, wf, ci=ci: e.dma_start(
                out=wf[:, 0:D],
                in_=wproj_d.ap()[ci * 128:(ci + 1) * 128, :]), D, sts)
        prep_q = []
        TPC = D // 128
        for ci in range(NC):
            for s4 in range(F // D):
                prep_q.append((
                    lambda e, wfv, ci=ci, s4=s4: e.dma_start(
                        out=wfv,
                        in_=w1_d.ap()[ci * 128:(ci + 1) * 128,
                                      s4 * D:(s4 + 1) * D]),
                    lambda e, hb, ci=ci, s4=s4: e.dma_start(
                        out=w1_t.ap()[s4 * TPC:(s4 + 1) * TPC,
                                      ci, :, :].rearrange("t p j -> p t j"),
                        in_=hb[:, 0:D].rearrange("p (t j) -> p t j", j=128)),
                    "w1t"))
        for gch in range(NF):
            prep_q.append((
                lambda e, wfv, gch=gch: e.dma_start(
                    out=wfv, in_=w2_d.ap()[gch * 128:(gch + 1) * 128, 0:D]),
                lambda e, hb, gch=gch: e.dma_start(
                    out=w2_r.ap()[gch * 128:(gch + 1) * 128, :],
                    in_=hb[:, 0:D]),
                "w2r"))
        prep_i = [0]
        wfvs = [wst[i].bitcast(F32)[:, 0:D] for i in range(2)]

        def drain_prep(n):
            for _ in range(n):
                if not prep_q:
                    return
                ldf, stf, dk = prep_q.pop(0)
                s = prep_i[0] % 2
                prep_i[0] += 1
                wk, hk = f"wst{s}", f"hpb{s}"
                p.dma("sy", [wk], [wk],
                      lambda e, ldf=ldf, s=s: ldf(e, wfvs[s]), "dW")
                p.c("ve", [wk], [hk], lambda e, s=s: e.tensor_copy(
                    out=hp_b[s][:, 0:D], in_=wfvs[s]))
                p.dma("gp", [hk], [dk],
                      lambda e, stf=stf, s=s: stf(e, hp_b[s]), "dS")

        p.barrier()

        for r in range(R):
            xrow = x_d.ap()[r * T:(r + 1) * T, :]
            yrow = y_d.ap()[r * T:(r + 1) * T, :]

            p.dma("sy", [], ["wv_all", "zgT", "y2f"], lambda e: e.dma_start(
                out=wv_all.rearrange("p (c d) -> p c d", d=D),
                in_=wv_r.ap().rearrange("(c p) d -> p c d", p=128)), "dWv")
            for h in range(H):
                p.c("ve", [], ["vones"] + (["zgT", "y2f"] if h == 0 else []),
                    lambda e, h=h: e.memset(
                    vsb.rearrange("p (t w) -> p t w", w=VW)[
                        :, :, h * (HD + 1) + HD:h * (HD + 1) + HD + 1], 1.0))

            def emit_vgen(tt, vp, vpk):
                for ci in range(NC):
                    for hv in range(NDH):
                        wl = [vpk] if ci in (0, NC - 1) and hv in (0, NDH - 1) else []
                        if tt <= 1 and ci == 0 and hv == 0:
                            wl = wl + ["PDh0", "PDh1"]
                        p.c("te", ["wv_all", "h1T"], wl,
                            lambda e, ci=ci, tt=tt, hv=hv, vp=vp: e.matmul(
                                out=vp[:, hv * 512:(hv + 1) * 512],
                                lhsT=h1T[:, ci * T + tt * 128:ci * T + (tt + 1) * 128],
                                rhs=wv_all[:, ci * D + hv * 512:
                                           ci * D + hv * 512 + 512],
                                start=(ci == 0), stop=(ci == NC - 1)))
                vdst = vsb[:, tt * VW:(tt + 1) * VW].rearrange(
                    "p (h w) -> p h w", w=HD + 1)[:, :, 0:HD]
                vsrc = vp[:, 0:D].rearrange("p (h w) -> p h w", w=HD)
                ex_v = ["zgT", "y2f"] if tt <= 1 else []
                p.c("se", [vpk, "vones"], ["vsb", vpk] + ex_v,
                    lambda e, vdst=vdst, vsrc=vsrc: e.copy(out=vdst, in_=vsrc))

            for tt in range(NT):
                xb, xk = xin[tt % 2], f"xin{tt % 2}"
                hb, hk = hp_b[tt % 2], f"hpb{tt % 2}"
                p.dma("sy", [xk], [xk], lambda e, tt=tt, xb=xb, xrow=xrow:
                      e.dma_start(out=xb,
                                  in_=xrow[tt * 128:(tt + 1) * 128, :]),
                      "dXa" if tt % 2 == 0 else "dXb")
                sl = tt % 2
                ln_stats([xk], xb, sl)
                p.c("ve", [xk, f"mcol{sl}", f"rcol{sl}"], [hk],
                    lambda e, xb=xb, hb=hb, sl=sl: e.tensor_scalar(
                        out=hb, in0=xb, scalar1=mcol[sl], scalar2=rcol[sl],
                        op0=ALU.subtract, op1=ALU.mult))
                half = (tt % 2) * 1024
                hfk = f"PAh{tt % 2}"
                for ci in range(NC):
                    ex_t = ["PA"] if (tt == 0 and ci == 0) else []
                    p.c("te", [hk, "ident_b"], [hfk] + ex_t,
                        lambda e, hb=hb, ci=ci, half=half: e.transpose(
                            out=PAb[:, half + ci * 128:half + (ci + 1) * 128],
                            in_=hb[:, ci * 128:(ci + 1) * 128],
                            identity=ident_b))
                dst = h1T.rearrange("p (c t) -> p c t", c=NC)[
                    :, :, tt * 128:(tt + 1) * 128]
                srcp = PAb[:, half:half + NC * 128].rearrange(
                    "p (c j) -> p c j", j=128)
                ex_h = (["srep", "selT", "zgT", "y2f", "usb"] if tt == 0 else [])
                p.c("ve", [hfk], ["h1T", hfk] + ex_h,
                    lambda e, dst=dst, srcp=srcp: e.tensor_copy(out=dst, in_=srcp))
                if tt % 2 == 1:
                    emit_vgen(tt - 1, PC, "PC")
                    emit_vgen(tt, PD, "PD")

            for et in range(ET):
                wb, wk = wst[et % 2], f"wst{et % 2}"
                p.dma("sy", [wk], [wk], lambda e, et=et, wb=wb: e.dma_start(
                    out=wb[:, 0:NC * 128].rearrange("p (c j) -> p c j", j=128),
                    in_=wqk_t.ap()[ET + et, :, :, :].rearrange(
                        "c p j -> p c j")), "dWka" if et % 2 == 0 else "dWkb")
                for qc4 in range(T // 512):
                    pk = PK[qc4 % 2]
                    ps = PS[qc4 % 2]
                    for ci in range(NC):
                        wl = [pk] if ci in (0, NC - 1) else []
                        if et == 0 and qc4 == 0 and ci == 0:
                            wl = wl + ["PAh0", "PAh1"]
                        p.c("te", [wk, "h1T"], wl,
                            lambda e, wb=wb, ci=ci, qc4=qc4, ps=ps: e.matmul(
                                out=ps[:, 0:512],
                                lhsT=wb[:, ci * 128:(ci + 1) * 128],
                                rhs=h1T[:, ci * T + qc4 * 512:ci * T + (qc4 + 1) * 512],
                                start=(ci == 0), stop=(ci == NC - 1)))
                    p.c("se", [pk, "kqb1"], ["kqb1", pk],
                        lambda e, ps=ps, qc4=qc4: e.copy(
                            out=kqb[1][:, qc4 * 512:(qc4 + 1) * 512],
                            in_=ps[:, 0:512]))
                p.dma("gp", ["kqb1"], ["kqb1", "kdram"], lambda e, et=et:
                      e.dma_start(out=k_dram.ap()[et, :, 0:T],
                                  in_=kqb[1][:, 0:T]), "dK")
            for et in range(ET):
                wb, wk = wst[et % 2], f"wst{et % 2}"
                p.dma("sy", [wk], [wk], lambda e, et=et, wb=wb: e.dma_start(
                    out=wb[:, 0:NC * 128].rearrange("p (c j) -> p c j", j=128),
                    in_=wqk_t.ap()[et, :, :, :].rearrange("c p j -> p c j")),
                    "dWqa" if et % 2 == 0 else "dWqb")
                for qc in range(NQC):
                    qpk = "PC" if qc % 2 == 0 else "PD"
                    qps = PC if qc % 2 == 0 else PD
                    for ci in range(NC):
                        p.c("te", [wk, "h1T"],
                            [qpk] if ci in (0, NC - 1) else [],
                            lambda e, wb=wb, ci=ci, qc=qc, qps=qps: e.matmul(
                                out=qps[:, 0:512],
                                lhsT=wb[:, ci * 128:(ci + 1) * 128],
                                rhs=h1T[:, ci * T + qc * 512:ci * T + (qc + 1) * 512],
                                start=(ci == 0), stop=(ci == NC - 1)))
                    kb, kk = kqb[0], "kqb0"
                    p.c("se", [qpk, kk + ("w" if True else "")], [kk, qpk],
                        lambda e, kb=kb, qc=qc, qps=qps: e.copy(
                            out=kb[:, qc * 512:(qc + 1) * 512], in_=qps[:, 0:512]))
                p.dma("gp", ["kqb0"], ["kqb0", "qbd"], lambda e, et=et:
                      e.dma_start(
                          out=qb_d.ap()[:, :, et * 512:(et + 1) * 512].rearrange(
                              "q p d -> p q d"),
                          in_=kqb[0].rearrange("p (q d) -> p q d", d=512)), "dQ")

            qsbs = [qst, qst2]

            def emit_qst_load(qc):
                qsb = qsbs[qc % 2]
                p.dma("sy", ["qbd", f"qst{qc % 2}"], [f"qst{qc % 2}"],
                      lambda e, qc=qc, qsb=qsb: e.dma_start(
                          out=qsb, in_=qb_d.ap()[qc, :, :]), "dQ2")

            def emit_kst_load(qc, hp):
                nkt = qc * DKT + DKT
                ks, kk = kst[hp % 2], f"kst{hp % 2}"
                p.dma("sy", ["kdram", kk], [kk], lambda e, hp=hp, nkt=nkt, ks=ks:
                      e.dma_start(out=ks[:, 0:nkt * 128],
                                  in_=k_dram.ap()[hp, :, 0:nkt * 128]), "dK2")

            emit_qst_load(0)
            emit_kst_load(0, 0)

            p.barrier()

            p.dma("sy", [], ["wpj", "h1T"], lambda e: e.dma_start(
                out=wpj[0:64, :].rearrange("r (m d) -> r m d", d=D),
                in_=wp_h.ap().rearrange("(m two) r d -> two r m d", two=2)[0]),
                "dWp")
            p.dma("sy", [], ["wpj", "h1T"], lambda e: e.dma_start(
                out=wpj[64:128, :].rearrange("r (m d) -> r m d", d=D),
                in_=wp_h.ap().rearrange("(m two) r d -> two r m d", two=2)[1]),
                "dWp")

            pend_se = [None]
            pend_rest = [None]

            def flush_se():
                if pend_se[0] is not None:
                    pend_se[0]()
                    pend_se[0] = None

            def flush_rest():
                flush_se()
                if pend_rest[0] is not None:
                    pend_rest[0]()
                    pend_rest[0] = None

            for qc in range(NQC):
                qsb = qsbs[qc % 2]
                qsk = f"qst{qc % 2}"
                nkt = qc * DKT + DKT
                for hp in range(NHP):
                    et = hp
                    ks, kk = kst[hp % 2], f"kst{hp % 2}"
                    if hp > 0:
                        emit_kst_load(qc, hp)
                    if r == 0:
                        drain_prep(2)
                    pvp = PC if hp % 2 == 0 else PD
                    pvk = "PC" if hp % 2 == 0 else "PD"

                    def emit_scores(kt, nkt=nkt, et=et, ks=ks, kk=kk, hp=hp,
                                    qsb=qsb, qsk=qsk):
                        diag = kt >= nkt - DKT
                        spk = PK[kt % 2]
                        sp = PS[kt % 2]
                        for hh in range(2):
                            p.c("te", [kk, qsk], [spk],
                                lambda e, kt=kt, hh=hh, sp=sp, et=et, ks=ks,
                                qsb=qsb: e.matmul(
                                    out=sp[:, hh * 512:(hh + 1) * 512],
                                    lhsT=ks[hh * 64:hh * 64 + 64,
                                            kt * 128:(kt + 1) * 128],
                                    rhs=qsb[hh * 64:hh * 64 + 64,
                                            et * 512:(et + 1) * 512],
                                    start=True, stop=True,
                                    tile_position=(hh * 64, 0)))
                        atb = attb[kt % 6]
                        atk = f"attb{kt % 6}"
                        p.c("se", [spk, "cpool2"], [atk, spk, W8A[kt % 6]],
                            lambda e, sp=sp, atb=atb: e.activation(
                                out=atb, in_=sp, func=AF.Exp, scale=0.125,
                                bias=cpool[:, 1:2]))
                        if diag:
                            rl = kt - (nkt - DKT)
                            for hh in range(2):
                                p.c("ve", [atk, "masks"], [atk],
                                    lambda e, rl=rl, hh=hh, atb=atb: e.tensor_tensor(
                                        out=atb[:, hh * 512:(hh + 1) * 512],
                                        in0=atb[:, hh * 512:(hh + 1) * 512],
                                        in1=masks[:, (DKT - 1 - rl) * 128:
                                                  (DKT - 1 - rl) * 128 + QCW],
                                        op=ALU.mult))

                    def emit_pv(kt, nkt=nkt, hp=hp, pvp=pvp, pvk=pvk):
                        atb = attb[kt % 6]
                        atk = f"attb{kt % 6}"
                        for hh in range(2):
                            vt = vsb[:, kt * VW + (2 * hp + hh) * (HD + 1):
                                     kt * VW + (2 * hp + hh) * (HD + 1) + HD + 1]
                            p.c("te", ["vsb", atk],
                                [pvk] if ((kt == 0 and hh == 0) or
                                          (kt == nkt - 1 and hh == 1)) else [],
                                lambda e, vt=vt, hh=hh, atb=atb,
                                kt=kt, nkt=nkt, pvp=pvp: e.matmul(
                                    out=pvp[0:65, hh * 512:(hh + 1) * 512],
                                    lhsT=vt,
                                    rhs=atb[:, hh * 512:(hh + 1) * 512],
                                    start=(kt == 0), stop=(kt == nkt - 1)))

                    emit_scores(0)
                    emit_scores(1)
                    for ktp in range(2, nkt, 2):
                        emit_scores(ktp)
                        emit_scores(ktp + 1)
                        if ktp >= 4:
                            emit_pv(ktp - 4)
                            emit_pv(ktp - 3)
                        if ktp == 2:
                            flush_se()
                        if ktp == 4:
                            flush_rest()
                    for kt in range(max(0, nkt - 4), nkt):
                        emit_pv(kt)
                    flush_rest()

                    def make_norm(hp=hp, pvp=pvp, pvk=pvk):
                        op = "PA"
                        opp = PA

                        def norm_se():
                            p.c("se", [pvk], ["recb"],
                                lambda e: e.activation(
                                    out=recb[0:1, 0:1024],
                                    in_=pvp[64:65, 0:1024], func=AF.Ln))
                            p.c("se", ["recb"], ["recbb"],
                                lambda e: e.activation(
                                    out=recbb, in_=recb[0:1, 0:1024],
                                    func=AF.Exp, scale=-1.0))

                        def norm_rest():
                            for hh in range(2):
                                p.c("te", ["recbb", "onesb"], [op],
                                    lambda e, hh=hh: e.matmul(
                                        out=opp[0:64, hh * 512:(hh + 1) * 512],
                                        lhsT=onesb[:, 0:64],
                                        rhs=recbb[0:1, hh * 512:(hh + 1) * 512],
                                        start=True, stop=True))
                            p.c("ve", [op], ["repb", op], lambda e: e.tensor_copy(
                                out=repb[0:64, 0:1024], in_=opp[0:64, 0:1024]))
                            p.c("ve", [pvk, "repb"], [f"ao2e{hp}", pvk],
                                lambda e, hp=hp: e.tensor_tensor(
                                    out=ao[0:64, hp * QCW:(hp + 1) * QCW],
                                    in0=pvp[0:64, 0:QCW],
                                    in1=repb[0:64, 0:QCW], op=ALU.mult))
                            p.c("ve", [pvk, "repb"], [f"aostg{hp}", pvk],
                                lambda e, hp=hp: e.tensor_tensor(
                                    out=aostg[0:64, hp * QCW:(hp + 1) * QCW],
                                    in0=pvp[0:64, QCW:2 * QCW],
                                    in1=repb[0:64, QCW:2 * QCW], op=ALU.mult))
                            p.dma("gp", [f"aostg{hp}"], [f"ao2o{hp}"],
                                  lambda e, hp=hp: e.dma_start(
                                      out=ao[64:128, hp * QCW:(hp + 1) * QCW],
                                      in_=aostg[0:64, hp * QCW:(hp + 1) * QCW]),
                                  "dX2")
                        return norm_se, norm_rest

                    pend_se[0], pend_rest[0] = make_norm()
                flush_se()
                if qc + 1 < NQC:
                    emit_qst_load(qc + 1)
                    emit_kst_load(qc + 1, 0)

                for ttl in range(4):
                    tt = qc * 4 + ttl
                    pk = PK[(ttl + 1) % 2]
                    ps = PS[(ttl + 1) % 2]
                    pkO = PK[2 + ttl % 2]
                    psO = PS[2 + ttl % 2]
                    for m in range(H // 2):
                        if ttl == 0 and m == 1:
                            flush_rest()
                        for hv in range(NDH):
                            last = (m in (0, H // 2 - 1) and hv in (0, NDH - 1))
                            p.c("te", [f"ao2e{m}", "wpj"], [pk] if last else [],
                                lambda e, m=m, hv=hv, ttl=ttl, ps=ps: e.matmul(
                                    out=ps[:, hv * 512:(hv + 1) * 512],
                                    lhsT=ao[0:64, m * QCW + ttl * 128:
                                            m * QCW + ttl * 128 + 128],
                                    rhs=wpj[0:64, m * D + hv * 512:
                                            m * D + hv * 512 + 512],
                                    start=(m == 0), stop=(m == H // 2 - 1),
                                    tile_position=(0, 0)))
                            p.c("te", [f"ao2o{m}", "wpj"], [pkO] if last else [],
                                lambda e, m=m, hv=hv, ttl=ttl, psO=psO: e.matmul(
                                    out=psO[:, hv * 512:(hv + 1) * 512],
                                    lhsT=ao[64:128, m * QCW + ttl * 128:
                                            m * QCW + ttl * 128 + 128],
                                    rhs=wpj[64:128, m * D + hv * 512:
                                            m * D + hv * 512 + 512],
                                    start=(m == 0), stop=(m == H // 2 - 1),
                                    tile_position=(64, 0)))
                    xb, xk = xin[tt % 2], f"xin{tt % 2}"
                    p.dma("sy", [xk], [xk], lambda e, tt=tt, xb=xb, xrow=xrow: e.dma_start(
                        out=xb, in_=xrow[tt * 128:(tt + 1) * 128, :]),
                        "dXa" if tt % 2 == 0 else "dXb")
                    sl = ttl % 2
                    tf, tfk = tmpf[sl], f"tmpfa{sl}"
                    tf2, tf2k = tmpf2[sl], f"tmpfb{sl}"
                    p.c("se", [pk], [tfk, pk],
                        lambda e, ps=ps, tf=tf: e.activation(
                            out=tf, in_=ps[:, 0:D], func=AF.Identity))
                    p.c("ve", [pkO, tfk], [tfk, pkO],
                        lambda e, psO=psO, tf=tf: e.tensor_tensor(
                            out=tf, in0=psO[:, 0:D], in1=tf, op=ALU.add))
                    ln_stats([tfk], tf, sl)
                    p.c("ve", [tfk, f"mcol{sl}", f"rcol{sl}"], [tfk],
                        lambda e, sl=sl, tf=tf: e.tensor_scalar(
                            out=tf, in0=tf, scalar1=mcol[sl],
                            scalar2=rcol[sl],
                            op0=ALU.subtract, op1=ALU.mult))
                    p.c("ve", [tfk, xk], [tf2k, pk],
                        lambda e, xb=xb, tf=tf, tf2=tf2: e.tensor_tensor(
                            out=tf2, in0=tf, in1=xb, op=ALU.add))
                    p.dma("gp", [tf2k], [], lambda e, tt=tt, tf2=tf2: e.dma_start(
                        out=x2_d.ap()[tt * 128:(tt + 1) * 128, :], in_=tf2), "dX2")
                    ln_stats([tf2k], tf2, sl)
                    p.c("ve", [tf2k, f"mcol{sl}", f"rcol{sl}"], [tfk],
                        lambda e, sl=sl, tf=tf, tf2=tf2: e.tensor_scalar(
                            out=tf, in0=tf2, scalar1=mcol[sl], scalar2=rcol[sl],
                            op0=ALU.subtract, op1=ALU.mult))
                    p.c("ve", [tfk], ["h2"], lambda e, tt=tt, tf=tf: e.tensor_copy(
                        out=h2[:, tt * D:(tt + 1) * D], in_=tf))
                    p.c("ve", [tfk, "wr_rep"], ["scol", "attb0", W8A[0]],
                        lambda e, tt=tt, tf=tf: e.scalar_tensor_tensor(
                            out=attb[0][:, 0:D], in0=tf, scalar=1.0,
                            in1=wr_rep, op0=ALU.mult, op1=ALU.mult,
                            accum_out=scol[:, tt:tt + 1]))

            if r == 0:
                drain_prep(len(prep_q))
            p.barrier()

            p.c("te", ["scol", "ident_f"], ["PD"], lambda e: e.transpose(
                out=PD[0:NT, 0:128], in_=scol, identity=ident_f))
            p.c("ve", ["PD"], ["st16", "PD"], lambda e: e.tensor_copy(
                out=st16, in_=PD[0:NT, 0:128]))
            p.dma("gp", ["st16"], ["sr_dram"], lambda e: e.dma_start(
                out=sr_d.ap().rearrange("o (t q) -> (o t) q", q=128),
                in_=st16), "dSR")
            p.dma("sy", ["sr_dram"], ["srow", "wpj"], lambda e: e.dma_start(
                out=srow, in_=sr_d.ap()), "dSR2")
            for h in range(T // 512):
                pk = PK[h % 2]
                p.c("te", ["srow", "onesf"], [pk], lambda e, h=h: e.matmul(
                    out=PS[h % 2][:, 0:512], lhsT=onesf,
                    rhs=srow[0:1, h * 512:(h + 1) * 512], start=True, stop=True))
                p.c("ve", [pk], ["srep", pk], lambda e, h=h: e.tensor_copy(
                    out=srep[:, h * 512:(h + 1) * 512], in_=PS[h % 2][:, 0:512]))
            NRC = (T + 1023) // 1024
            for tt in range(NT):
                for c in range(NRC):
                    dst = rankc[:, tt:tt + 1] if c == 0 else tmpc[:, 0:1]
                    dk = "rankc" if c == 0 else "tmpc"
                    pk = PK[c % 2]
                    w = min(1024, T - c * 1024)
                    p.c("ve", ["srep", "scol"], [dk, pk],
                        lambda e, tt=tt, c=c, dst=dst, w=w, pk=pk: e.tensor_scalar(
                            out=PS[PK.index(pk)][:, 0:w],
                            in0=srep[:, c * 1024:c * 1024 + w],
                            scalar1=scol[:, tt:tt + 1],
                            scalar2=0.0, op0=ALU.is_gt, op1=ALU.add,
                            accum_out=dst))
                if NRC > 1:
                    p.c("ve", ["rankc", "tmpc"], ["rankc"],
                        lambda e, tt=tt: e.tensor_tensor(
                            out=rankc[:, tt:tt + 1], in0=rankc[:, tt:tt + 1],
                            in1=tmpc[:, 0:1], op=ALU.add))
            p.c("ve", ["rankc"], ["tmpc"], lambda e: e.tensor_scalar(
                out=tmpc[:, 0:NT], in0=rankc[:, 0:NT], scalar1=float(KSEL),
                scalar2=1.0e6, op0=ALU.is_ge, op1=ALU.mult))
            p.c("ve", ["rankc", "tmpc"], ["sslot"], lambda e: e.tensor_tensor(
                out=sslot[:, 0:NT], in0=rankc[:, 0:NT], in1=tmpc[:, 0:NT],
                op=ALU.add))
            p.c("se", ["scol"], ["wcol"], lambda e: e.activation(
                out=wcol[:, 0:NT], in_=scol[:, 0:NT], func=AF.Tanh, scale=0.5))
            p.c("ve", ["wcol"], ["wcol"], lambda e: e.tensor_scalar(
                out=wcol[:, 0:NT], in0=wcol[:, 0:NT], scalar1=0.5, scalar2=0.5,
                op0=ALU.mult, op1=ALU.add))
            p.c("ve", ["rankc"], ["tmpc"], lambda e: e.tensor_scalar(
                out=tmpc[:, 0:NT], in0=rankc[:, 0:NT], scalar1=float(KSEL),
                scalar2=None, op0=ALU.is_lt))
            p.c("ve", ["wcol", "tmpc"], ["wcol"], lambda e: e.tensor_tensor(
                out=wcol[:, 0:NT], in0=wcol[:, 0:NT], in1=tmpc[:, 0:NT],
                op=ALU.mult))


            cgs = [list(range(gstart, min(gstart + 3, NC)))
                   for gstart in range(0, NC, 3)]
            first_sel = [True]
            for cg in cgs:
                for tt in range(NT):
                    pt, ptk = ptb[tt % 2], f"ptb{tt % 2}"
                    p.c("ve", ["sslot", "iota"], [ptk, W8A[tt % 2]],
                        lambda e, tt=tt, pt=pt: e.tensor_scalar(
                            out=pt, in0=iota[:, 0:KSEL],
                            scalar1=sslot[:, tt:tt + 1], scalar2=None,
                            op0=ALU.is_equal))
                    if cg[0] == 0:
                        for hs in range(KH):
                            p.c("te", [ptk, "iotacb"],
                                ["PD"] if (tt in (0, NT - 1) and hs in (0, KH - 1)) else [],
                                lambda e, tt=tt, hs=hs, pt=pt: e.matmul(
                                    out=PD[0:1, hs * 512:(hs + 1) * 512],
                                    lhsT=iotacb,
                                    rhs=pt[:, hs * 512:(hs + 1) * 512],
                                    start=(tt == 0), stop=False))
                            p.c("te", [ptk, "toffb"], [],
                                lambda e, tt=tt, hs=hs, pt=pt: e.matmul(
                                    out=PD[0:1, hs * 512:(hs + 1) * 512],
                                    lhsT=toffb[:, tt:tt + 1],
                                    rhs=pt[:, hs * 512:(hs + 1) * 512],
                                    start=False, stop=(tt == NT - 1)))
                    for gi, ci in enumerate(cg):
                        pk = PK[gi]
                        ps = PS[gi]
                        for hs in range(KH):
                            p.c("te", ["h2", ptk],
                                [pk] if (tt in (0, NT - 1) and hs in (0, KH - 1)) else [],
                                lambda e, tt=tt, ci=ci, hs=hs, ps=ps, pt=pt: e.matmul(
                                    out=ps[:, hs * 512:(hs + 1) * 512],
                                    lhsT=h2[:, tt * D + ci * 128:
                                            tt * D + (ci + 1) * 128],
                                    rhs=pt[:, hs * 512:(hs + 1) * 512],
                                    start=(tt == 0), stop=(tt == NT - 1)))
                for gi, ci in enumerate(cg):
                    pk = PK[gi]
                    ps = PS[gi]
                    extra = ["srow", "srep"] if first_sel[0] else []
                    first_sel[0] = False
                    p.c("ve", [pk], ["selT", pk] + extra,
                        lambda e, ci=ci, ps=ps: e.tensor_copy(
                            out=selT[:, ci * KSEL:(ci + 1) * KSEL],
                            in_=ps[:, 0:KSEL]))

            p.c("ve", ["PD"], ["tofr", "PD", "srow"], lambda e: e.tensor_copy(
                out=tofr, in_=PD[0:1, 0:KSEL]))
            p.dma("gp", ["tofr"], ["tofdram"], lambda e: e.dma_start(
                out=tof_d.ap().rearrange("(o k) w -> o (k w)", o=1),
                in_=tofr), "dTF")
            for jt in range(NJ):
                p.dma("sy", ["tofdram"], ["tofcf"], lambda e, jt=jt: e.dma_start(
                    out=tofcf[:, jt:jt + 1],
                    in_=tof_d.ap()[jt * 128:(jt + 1) * 128, :]), "dTG")
            p.c("ve", ["tofcf"], ["tofc"], lambda e: e.tensor_copy(
                out=tofc, in_=tofcf))

            for ft in range(NF):
                wb, wk = wst[ft % 2], f"wst{ft % 2}"
                p.dma("sy", ["w1t", wk], [wk], lambda e, ft=ft, wb=wb: e.dma_start(
                    out=wb[:, 0:NC * 128].rearrange("p (c j) -> p c j", j=128),
                    in_=w1_t.ap()[ft, :, :, :].rearrange("c p j -> p c j")),
                    "dW1a" if ft % 2 == 0 else "dW1b")
                pk = PK[ft % 2]
                ps = PS[ft % 2]
                for ci in range(NC):
                    for hs in range(KH):
                        p.c("te", [wk, "selT"],
                            [pk] if (ci in (0, NC - 1) and hs in (0, KH - 1)) else [],
                            lambda e, wb=wb, ci=ci, hs=hs, ps=ps: e.matmul(
                                out=ps[:, hs * 512:(hs + 1) * 512],
                                lhsT=wb[:, ci * 128:(ci + 1) * 128],
                                rhs=selT[:, ci * KSEL + hs * 512:
                                         ci * KSEL + hs * 512 + 512],
                                start=(ci == 0), stop=(ci == NC - 1)))
                p.c("se", [pk, "b1c"], ["zgT", pk],
                    lambda e, ft=ft, ps=ps: e.activation(
                        out=zgT[:, ft * KSEL:(ft + 1) * KSEL],
                        in_=ps[:, 0:KSEL], func=AF.Gelu_apprx_tanh,
                        bias=b1c[:, ft:ft + 1]))

            JH = max(1, NJ // 4)
            for jh in range(JH):
                jts = list(range(jh * NJ // JH, (jh + 1) * NJ // JH))
                for ft in range(NF):
                    wb, wk = wst[ft % 2], f"wst{ft % 2}"
                    p.dma("sy", ["w2r", wk], [wk], lambda e, ft=ft, wb=wb: e.dma_start(
                        out=wb[:, 0:D],
                        in_=w2_r.ap()[ft * 128:(ft + 1) * 128, :]),
                        "dW2a" if ft % 2 == 0 else "dW2b")
                    for ji, jt in enumerate(jts):
                        pk = PK[ji]
                        ps = PS[ji]
                        for hv in range(NDH):
                            p.c("te", [wk, "zgT"],
                                [pk] if (ft == 0 and hv == 0) else [],
                                lambda e, wb=wb, ft=ft, jt=jt, hv=hv, ps=ps: e.matmul(
                                    out=ps[:, hv * 512:(hv + 1) * 512],
                                    lhsT=zgT[:, ft * KSEL + jt * 128:
                                             ft * KSEL + (jt + 1) * 128],
                                    rhs=wb[:, hv * 512:(hv + 1) * 512],
                                    start=(ft == 0), stop=False))
                for ji, jt in enumerate(jts):
                    pk = PK[ji]
                    ps = PS[ji]
                    for hv in range(NDH):
                        p.c("te", ["onesb", "b2row"], [pk],
                            lambda e, hv=hv, ps=ps: e.matmul(
                                out=ps[:, hv * 512:(hv + 1) * 512],
                                lhsT=onesb, rhs=b2row[:, hv * 512:(hv + 1) * 512],
                                start=False, stop=True))
                    extra_u = ["selT"] if (jh == 0 and ji == 0) else []
                    p.c("ve", [pk], ["usb", pk] + extra_u,
                        lambda e, jt=jt, ps=ps: e.tensor_copy(
                            out=usb[:, jt * D:(jt + 1) * D], in_=ps[:, 0:D]))

            for jt in range(NJ):
                p.dma("gp", ["tofc", "usb"], ["updd"],
                      lambda e, jt=jt: e.indirect_dma_start(
                          out=upd_d.ap(),
                          out_offset=IndirectOffsetOnAxis(tofc[:, jt:jt + 1], 0),
                          in_=usb[:, jt * D:(jt + 1) * D], in_offset=None,
                          oob_is_err=False, bounds_check=T - 1), "dUS")

            for tt in range(NT):
                ub, uk = hp_b[tt % 2], f"hpb{tt % 2}"
                p.dma("sy", ["updd", uk], [uk], lambda e, tt=tt, ub=ub:
                      e.dma_start(out=ub[:, 0:D],
                                  in_=upd_d.ap()[tt * 128:(tt + 1) * 128, :]),
                      "dUPa" if tt % 2 == 0 else "dUPb")
                sl = tt % 2
                tf, tfk = tmpf[sl], f"tmpfa{sl}"
                tf2, tf2k = tmpf2[sl], f"tmpfb{sl}"
                p.c("ve", [uk, "wcol", "h2"], [tf2k],
                    lambda e, tt=tt, ub=ub, tf2=tf2: e.scalar_tensor_tensor(
                        out=tf2, in0=ub[:, 0:D],
                        scalar=wcol[:, tt:tt + 1],
                        in1=h2[:, tt * D:(tt + 1) * D],
                        op0=ALU.mult, op1=ALU.add))
                xb, xk = x2in[tt % 2], f"xin{tt % 2}"
                p.dma("sy", [xk], [xk], lambda e, tt=tt, xb=xb: e.dma_start(
                    out=xb, in_=x2_d.ap()[tt * 128:(tt + 1) * 128, :]),
                    "dX2c" if tt % 2 == 0 else "dX2d")
                ln_stats([tf2k], tf2, sl)
                p.c("ve", [tf2k, f"mcol{sl}", f"rcol{sl}"], [tfk],
                    lambda e, sl=sl, tf=tf, tf2=tf2: e.tensor_scalar(
                        out=tf, in0=tf2, scalar1=mcol[sl], scalar2=rcol[sl],
                        op0=ALU.subtract, op1=ALU.mult))
                ob, ok = oute[tt % 2], f"oute{tt % 2}"
                if tt == 0:
                    extra = ["ptb0", "ptb1", "attb0", "attb1", W8A[0], W8A[1]]
                elif tt == 1:
                    extra = ["pjb0", "pjb1", "attb2", "attb3", W8A[2], W8A[3]]
                else:
                    extra = []
                p.c("gp", [tfk, xk], [ok] + extra,
                    lambda e, xb=xb, ob=ob, tf=tf: e.tensor_tensor(
                        out=ob[:, 0:D], in0=tf, in1=xb, op=ALU.add))
                p.dma("gp", [ok], [ok], lambda e, tt=tt, ob=ob, yrow=yrow: e.dma_start(
                    out=yrow[tt * 128:(tt + 1) * 128, :], in_=ob[:, 0:D]), "dO")

        p.barrier()
        p.c("sy", [], [], lambda e: e.nop())

        with nc.Block() as block:
            p.emit(nc, block, es)

    return nc


def kernel(L0_attn_x, L0_attn_w_qkv, L0_attn_w_proj, L0_ffn_w_router,
           L0_ffn_W1, L0_ffn_b1, L0_ffn_W2, L0_ffn_b2):
    cfg = default_cfg()
    T, D, F, R = cfg["T"], cfg["D"], cfg["F"], cfg["R"]
    ncores = 8
    x = np.ascontiguousarray(np.asarray(L0_attn_x, dtype=np.float32))
    wqkv = np.ascontiguousarray(np.asarray(L0_attn_w_qkv, dtype=np.float32))
    wproj = np.ascontiguousarray(np.asarray(L0_attn_w_proj, dtype=np.float32))
    wr = np.ascontiguousarray(
        np.asarray(L0_ffn_w_router, dtype=np.float32)).reshape(1, D)
    W1 = np.ascontiguousarray(np.asarray(L0_ffn_W1, dtype=np.float32))
    b1 = np.ascontiguousarray(np.asarray(L0_ffn_b1, dtype=np.float32)).reshape(1, F)
    W2 = np.ascontiguousarray(np.asarray(L0_ffn_W2, dtype=np.float32))
    b2 = np.ascontiguousarray(np.asarray(L0_ffn_b2, dtype=np.float32)).reshape(1, D)

    nc = build_program(cfg)
    in_maps = []
    for i in range(ncores):
        xs = np.ascontiguousarray(x[i * R:(i + 1) * R].reshape(R * T, D))
        in_maps.append({"x": xs, "w_qkv": wqkv, "w_proj": wproj,
                        "w_router": wr, "W1": W1, "b1": b1,
                        "W2": W2, "b2": b2})
    res = run_bass_kernel_spmd(nc, in_maps, core_ids=list(range(ncores)))
    outs = [res.results[i]["y"].reshape(R, T, D) for i in range(ncores)]
    return np.concatenate(outs, axis=0)


# version 34
# speedup vs baseline: 1.0064x; 1.0064x over previous
import numpy as np
import ml_dtypes
import concourse.bass as bass
import concourse.mybir as mybir
from concourse.bass import IndirectOffsetOnAxis
from concourse.bass_utils import run_bass_kernel_spmd

F32 = mybir.dt.float32
BF16 = mybir.dt.bfloat16
U8 = mybir.dt.uint8
I32 = mybir.dt.int32
AF = mybir.ActivationFunctionType
ALU = mybir.AluOpType

NEG = -1.0e9
EXPB = -20.0
EPS = 1.0e-5


class Prog:
    ENGINES = ("te", "ve", "se", "gp", "sy")

    class Op:
        __slots__ = ("idx", "eng", "fn", "kind", "dma_sem", "dma_idx",
                     "needs_inc", "cnt", "deps", "dep_dma")

    def __init__(self):
        self.ops = []
        self.dma_count = {}
        self.last_writer = {}
        self.readers = {}
        self.bar_ops = []
        self.bar_dma = {}

    def _record(self, eng, reads, writes, fn, kind, dma_sem=None):
        op = Prog.Op()
        op.idx = len(self.ops)
        op.eng = eng
        op.fn = fn
        op.kind = kind
        op.dma_sem = dma_sem
        op.needs_inc = False
        op.cnt = None
        op.deps = []
        op.dep_dma = {}

        deps = []
        for k in reads:
            w = self.last_writer.get(k)
            if w is not None:
                deps.append(w)
        for k in writes:
            w = self.last_writer.get(k)
            if w is not None:
                deps.append(w)
            deps.extend(self.readers.get(k, ()))
        deps.extend(self.bar_ops)

        seen = set()
        for d in deps:
            if d is op or id(d) in seen:
                continue
            seen.add(id(d))
            if d.kind == "d":
                want = 16 * d.dma_idx
                if op.dep_dma.get(d.dma_sem, 0) < want:
                    op.dep_dma[d.dma_sem] = want
            else:
                if d.eng == "te" and eng == "te":
                    continue
                d.needs_inc = True
                op.deps.append(d)
        for s, c in self.bar_dma.items():
            want = 16 * c
            if op.dep_dma.get(s, 0) < want:
                op.dep_dma[s] = want
        if kind == "d":
            prev = self.dma_count.get(dma_sem, 0)
            if prev > 0 and op.dep_dma.get(dma_sem, 0) < 16 * prev:
                op.dep_dma[dma_sem] = 16 * prev
            self.dma_count[dma_sem] = prev + 1
            op.dma_idx = prev + 1
        else:
            op.dma_idx = None

        for k in writes:
            self.last_writer[k] = op
            self.readers[k] = []
        for k in reads:
            self.readers.setdefault(k, []).append(op)
        self.ops.append(op)
        return op

    def c(self, eng, reads, writes, fn):
        return self._record(eng, reads, writes, fn, "c")

    def dma(self, eng, reads, writes, fn, sem):
        return self._record(eng, reads, writes, fn, "d", dma_sem=sem)

    def barrier(self):
        last = {}
        for op in self.ops:
            if op.kind == "c":
                last[op.eng] = op
        self.bar_ops = list(last.values())
        for op in self.bar_ops:
            op.needs_inc = True
        self.bar_dma = dict(self.dma_count)
        self.last_writer = {}
        self.readers = {}

    def emit(self, nc, block, es):
        cnt = {e: 0 for e in Prog.ENGINES}
        for op in self.ops:
            if op.kind == "c" and op.needs_inc:
                cnt[op.eng] += 1
                op.cnt = cnt[op.eng]
        sems = {e: es.enter_context(nc.semaphore(f"sem_{e}"))
                for e in Prog.ENGINES if cnt[e] > 0}
        dsems = {s: es.enter_context(nc.semaphore(f"dma_{s}"))
                 for s in self.dma_count}
        streams = {e: [op for op in self.ops if op.eng == e]
                   for e in Prog.ENGINES}

        def make_body(engname):
            ops = streams[engname]

            def body(eng):
                water = {}
                for op in ops:
                    for d in op.deps:
                        if water.get(d.eng, 0) < d.cnt:
                            water[d.eng] = d.cnt
                            eng.wait_ge(sems[d.eng], d.cnt)
                    for s, val in op.dep_dma.items():
                        if water.get("D" + s, 0) < val:
                            water["D" + s] = val
                            eng.wait_ge(dsems[s], val)
                    inst = op.fn(eng)
                    if op.kind == "d":
                        inst.then_inc(dsems[op.dma_sem], 16)
                    elif op.needs_inc:
                        inst.then_inc(sems[op.eng], 1)
            return body

        names = {"te": "tensor", "ve": "vector", "se": "scalar",
                 "gp": "gpsimd", "sy": "sync"}
        for e in Prog.ENGINES:
            if streams[e]:
                getattr(block, names[e])(make_body(e))


def default_cfg():
    return dict(T=2048, D=1024, F=4096, H=16, R=4)


def build_program(cfg):
    T, D, F, H, R = cfg["T"], cfg["D"], cfg["F"], cfg["H"], cfg["R"]
    HD = 64
    assert D == H * HD
    NT = T // 128
    NC = D // 128
    NF = F // 128
    NQC = T // 512
    QCW = 512
    KSEL = T // 2
    NJ = KSEL // 128
    NHP = H // 2
    ET = NC
    DKT = 4
    NDH = D // 512
    KH = KSEL // 512
    VW = H * (HD + 1)

    nc = bass.Bass("TRN2", target_bir_lowering=False)

    x_d = nc.dram_tensor("x", [R * T, D], F32, kind="ExternalInput")
    wqkv_d = nc.dram_tensor("w_qkv", [D, 3 * D], F32, kind="ExternalInput")
    wproj_d = nc.dram_tensor("w_proj", [D, D], F32, kind="ExternalInput")
    wr_d = nc.dram_tensor("w_router", [1, D], F32, kind="ExternalInput")
    w1_d = nc.dram_tensor("W1", [D, F], F32, kind="ExternalInput")
    b1_d = nc.dram_tensor("b1", [1, F], F32, kind="ExternalInput")
    w2_d = nc.dram_tensor("W2", [F, D], F32, kind="ExternalInput")
    b2_d = nc.dram_tensor("b2", [1, D], F32, kind="ExternalInput")
    y_d = nc.dram_tensor("y", [R * T, D], F32, kind="ExternalOutput")

    wqk_t = nc.dram_tensor("wqk_t", [2 * ET, NC, 128, 128], BF16, kind="Internal")
    wv_r = nc.dram_tensor("wv_r", [D, D], BF16, kind="Internal")
    wp_h = nc.dram_tensor("wp_h", [H, HD, D], BF16, kind="Internal")
    w1_t = nc.dram_tensor("w1_t", [NF, NC, 128, 128], BF16, kind="Internal")
    w2_r = nc.dram_tensor("w2_r", [F, D], BF16, kind="Internal")
    k_dram = nc.dram_tensor("k_dram", [ET, 128, T], BF16, kind="Internal")
    qb_d = nc.dram_tensor("qb_d", [NQC, 128, ET * 512], BF16, kind="Internal")
    x2_d = nc.dram_tensor("x2_d", [T, D], F32, kind="Internal")
    sr_d = nc.dram_tensor("sr_d", [1, T], F32, kind="Internal")
    tof_d = nc.dram_tensor("tof_d", [KSEL, 1], F32, kind="Internal")
    upd_d = nc.dram_tensor("upd_d", [T, D], BF16, kind="Internal")

    p = Prog()
    from contextlib import ExitStack
    es = ExitStack()
    with es:
        def sb(name, shape, dt):
            return es.enter_context(nc.sbuf_tensor(name, shape, dt)).ap()

        iota = sb("iota", [128, max(1024, KSEL)], F32)
        iotac = sb("iotac", [128, 1], F32)
        ident_f = sb("ident_f", [128, 128], F32)
        ident_b = sb("ident_b", [128, 128], BF16)
        masks = sb("masks", [128, (DKT - 1) * 128 + QCW], BF16)
        onesf = sb("onesf", [1, 128], F32)
        onesb = sb("onesb", [1, 128], BF16)
        cpool = sb("cpool", [128, 4], F32)
        wr_rep = sb("wr_rep", [128, D], F32)
        b1c = sb("b1c", [128, NF], F32)
        b2row = sb("b2row", [1, D], BF16)
        stat6 = [sb(f"stat6{i}", [128, 24], F32) for i in range(2)]
        stat2 = [sb(f"stat2{i}", [128, 2], F32) for i in range(2)]
        mcol = [sb(f"mcol{i}", [128, 1], F32) for i in range(2)]
        rcol = [sb(f"rcol{i}", [128, 1], F32) for i in range(2)]
        scol = sb("scol", [128, NT], F32)
        rankc = sb("rankc", [128, NT], F32)
        sslot = sb("sslot", [128, NT], F32)
        wcol = sb("wcol", [128, NT], F32)
        tmpc = sb("tmpc", [128, NT], F32)
        st16 = sb("st16", [NT, 128], F32)
        repb = sb("repb", [64, 1024], BF16)
        recbb = sb("recbb", [1, 1024], BF16)
        iotacb = sb("iotacb", [128, 1], BF16)
        toffb = sb("toffb", [128, NT], BF16)
        tofc = sb("tofc", [128, NJ], I32)
        tofcf = sb("tofcf", [128, NJ], F32)
        xin = [sb(f"xin{i}", [128, D], F32) for i in range(2)]
        x2in = [sb(f"x2in{i}", [128, D], F32) for i in range(2)]
        tmpf = [sb(f"tmpfa{i}", [128, D], F32) for i in range(2)]
        tmpf2 = [sb(f"tmpfb{i}", [128, D], F32) for i in range(2)]
        hp_b = [sb(f"hpb{i}", [128, D], BF16) for i in range(2)]
        wst = [sb(f"wst{i}", [128, 2048], BF16) for i in range(2)]
        kqb = [sb(f"kqb{i}", [128, 2048], BF16) for i in range(2)]
        qst = sb("qst", [128, ET * 512], BF16)
        h2 = sb("h2", [128, NT * D], BF16)

        w8 = sb("w8", [128, 8192], U8)
        attb = [w8[:, i * 2048:(i + 1) * 2048].bitcast(BF16) for i in range(4)]

        ptb = [w8[:, i * 2048:(i + 1) * 2048].bitcast(BF16)[:, 0:KSEL]
               for i in range(2)]
        pjb = [w8[:, 4096 + i * 2048:4096 + (i + 1) * 2048].bitcast(BF16)[:, 0:NJ * 128]
               for i in range(2)]
        oute = [w8[:, i * 4096:(i + 1) * 4096].bitcast(F32) for i in range(2)]
        W8A = ["w8a0", "w8a1", "w8b0", "w8b1", "w8g0", "w8g1"]

        GSZ_cast = 73728 + D * 4
        GSZ_B = NC * T * 2 + NT * VW * 2 + NC * D * 2
        GSZ_C = NC * T * 2 + NT * VW * 2 + H * QCW * 2 + 8192
        GSZ_D = T * 4 + NC * KSEL * 2 + max(NF * KSEL * 2, NT * D * 4)
        GSZ = max(GSZ_cast, GSZ_B, GSZ_C, GSZ_D)
        g = sb("g", [128, GSZ], U8)

        def gv(off, part, width, dt):
            bpe = 4 if dt == F32 else 2
            ap = g[:part, off:off + width * bpe].bitcast(dt)
            return ap
        wfp = [gv(i * 16384, 128, 4096, F32) for i in range(3)]
        wbp = [gv(49152 + i * 8192, 128, 4096, BF16) for i in range(3)]
        zbuf = gv(73728, 128, D, F32)
        w_f32 = wfp[0]
        w_bf = wbp[0]
        h1T = gv(0, 128, NC * T, BF16)
        vsb = gv(NC * T * 2, 128, NT * VW, BF16)
        _voff = NC * T * 2 + NT * VW * 2
        wpj = gv(0, 128, H // 2 * D, BF16)
        qst2 = gv(H // 2 * D * 2, 128, ET * 512, BF16)
        recb = gv(H // 2 * D * 2 + ET * 512 * 2, 1, 1024, F32)
        _goff = H // 2 * D * 2 + ET * 512 * 2 + 4096
        attg = [gv(_goff + i * 2048, 128, 1024, BF16) for i in range(2)]
        assert _goff + 4096 <= NC * T * 2
        attb = attb + attg
        ao = gv(_voff, 128, H // 2 * QCW, BF16)
        aostg = gv(_voff + H * QCW, 64, H // 2 * QCW, BF16)
        kst = [gv(_voff + H * QCW * 2 + i * 4096, 128, 2048, BF16)
               for i in range(2)]
        wv_all = gv(_voff, 128, NC * D, BF16)
        srow = gv(_voff + H * QCW * 2, 1, T, F32)
        tofr = gv(_voff + H * QCW * 2, 1, KSEL, F32)
        srep = gv(0, 128, T, F32)
        selT = gv(T * 4, 128, NC * KSEL, BF16)
        _zoff = T * 4 + NC * KSEL * 2
        zgT = gv(_zoff, 128, NF * KSEL, BF16)
        y2f = gv(_zoff, 128, NT * D, F32)
        usb = gv(T * 4, 128, NJ * D, BF16)
        assert _zoff + max(NF * KSEL * 2, NT * D * 4) <= GSZ
        assert T * 4 + NJ * D * 2 <= _zoff
        assert T * 4 <= _voff + H * QCW * 2

        PS = [es.enter_context(nc.psum_tensor(f"P{i}", [128, 1024], F32)).ap()
              for i in range(4)]
        PA, PB, PC, PD = PS
        PDb = PD.bitcast(BF16)
        PAb = PA.bitcast(BF16)
        PK = ["PA", "PB", "PC", "PD"]

        NST = (D + 511) // 512

        def ln_stats(srckeys, srcap, sl):
            s6, s2 = stat6[sl], stat2[sl]
            k6, k2, km, kr = f"stat6{sl}", f"stat2{sl}", f"mcol{sl}", f"rcol{sl}"
            for c in range(NST):
                p.c("ve", list(srckeys) + ([k6] if c else []), [k6],
                    lambda e, c=c: e.bn_stats(
                        out=s6[:, c * 6:(c + 1) * 6],
                        in_=srcap[:, c * 512:(c + 1) * 512]))
            p.c("ve", [k6], [k2], lambda e: e.bn_aggr(
                out=s2, in_=s6[:, 0:NST * 6]))
            p.c("ve", [k2], [km], lambda e: e.tensor_copy(
                out=mcol[sl], in_=s2[:, 0:1]))
            p.c("se", [k2, "cpool"], [kr], lambda e: e.activation(
                out=rcol[sl], in_=s2[:, 1:2], func=AF.Ln, bias=cpool[:, 0:1]))
            p.c("se", [kr], [kr], lambda e: e.activation(
                out=rcol[sl], in_=rcol[sl], func=AF.Exp, scale=-0.5))

        p.c("gp", [], ["iota"], lambda e: e.iota(
            iota, pattern=[[1, iota.shape[1]]], base=0, channel_multiplier=0,
            allow_small_or_imprecise_dtypes=True))
        p.c("gp", [], ["iotac"], lambda e: e.iota(
            iotac, pattern=[[0, 1]], base=0, channel_multiplier=1,
            allow_small_or_imprecise_dtypes=True))
        p.c("ve", ["iotac"], ["iotacb"], lambda e: e.tensor_copy(
            out=iotacb, in_=iotac))
        p.c("gp", [], ["toffb"], lambda e: e.iota(
            toffb, pattern=[[128, NT]], base=0, channel_multiplier=0,
            allow_small_or_imprecise_dtypes=True))
        p.c("ve", [], ["zb"], lambda e: e.memset(zbuf, 0.0))
        for i in range(NT):
            p.dma("gp", ["zb"], ["updd"], lambda e, i=i: e.dma_start(
                out=upd_d.ap()[i * 128:(i + 1) * 128, :],
                in_=zbuf), "dUZ")
        p.c("ve", [], ["cpool"], lambda e: e.memset(cpool[:, 0:1], EPS))
        p.c("ve", [], ["cpool2"], lambda e: e.memset(cpool[:, 1:2], EXPB))
        p.c("ve", [], ["onesf"], lambda e: e.memset(onesf, 1.0))
        p.c("ve", [], ["onesb"], lambda e: e.memset(onesb, 1.0))
        p.c("ve", ["iota", "iotac"], ["ident_f"], lambda e: e.tensor_scalar(
            out=ident_f, in0=iota[:, 0:128], scalar1=iotac, scalar2=None,
            op0=ALU.is_equal))
        p.c("ve", ["ident_f"], ["ident_b"], lambda e: e.tensor_copy(
            out=ident_b, in_=ident_f))
        MW = (DKT - 1) * 128
        p.c("ve", ["iotac"], ["tmpc"], lambda e: e.tensor_scalar(
            out=tmpc[:, 0:1], in0=iotac, scalar1=float(MW),
            scalar2=None, op0=ALU.add))
        p.c("ve", ["iota", "tmpc"], ["masks"], lambda e: e.tensor_scalar(
            out=masks, in0=iota[:, 0:MW + QCW],
            scalar1=tmpc[:, 0:1], scalar2=None, op0=ALU.is_ge))

        p.dma("sy", [], ["wf0"], lambda e: e.dma_start(
            out=w_f32[0:1, 0:D], in_=wr_d.ap()), "dW")
        for h in range(D // 512):
            pk = PK[h % 2]
            p.c("te", ["wf0", "onesf"], [pk], lambda e, h=h: e.matmul(
                out=PS[h % 2][:, 0:512], lhsT=onesf,
                rhs=w_f32[0:1, h * 512:(h + 1) * 512], start=True, stop=True))
            p.c("ve", [pk], ["wr_rep"], lambda e, h=h: e.tensor_copy(
                out=wr_rep[:, h * 512:(h + 1) * 512], in_=PS[h % 2][:, 0:512]))
        p.dma("sy", ["wr_rep"], ["wf0"], lambda e: e.dma_start(
            out=w_f32[0:1, 0:D], in_=b2_d.ap()), "dW")
        p.c("ve", ["wf0"], ["b2row"], lambda e: e.tensor_copy(
            out=b2row, in_=w_f32[0:1, 0:D]))
        p.c("ve", ["b2row"], ["wf0"], lambda e: e.memset(w_f32, 0.0))
        p.dma("sy", ["wf0"], ["wf0"], lambda e: e.dma_start(
            out=w_f32[0:1, 0:F], in_=b1_d.ap()), "dW")
        for ft in range(NF):
            pk = PK[2 + ft % 2]
            pp = PS[2 + ft % 2]
            p.c("te", ["wf0", "ident_f", pk], [pk],
                lambda e, ft=ft, pp=pp: e.transpose(
                    out=pp[:, 0:128], in_=w_f32[:, ft * 128:(ft + 1) * 128],
                    identity=ident_f))
            p.c("ve", [pk], ["b1c"], lambda e, ft=ft, pp=pp: e.tensor_copy(
                out=b1c[:, ft:ft + 1], in_=pp[:, 0:1]))

        _slot = [0]

        def prep_iter(load_fn, width, store_fns):
            s = _slot[0] % 3
            _slot[0] += 1
            wf, wfk = wfp[s], f"wf{s}"
            wb, wbk = wbp[s], f"wb{s}"
            p.dma("sy", [wfk], [wfk],
                  lambda e, load_fn=load_fn, wf=wf: load_fn(e, wf),
                  ["dW", "dWb", "dWc"][s])
            p.c("ve", [wfk], [wbk], lambda e, wb=wb, wf=wf, width=width:
                e.tensor_copy(out=wb[:, 0:width], in_=wf[:, 0:width]))
            for sf, dk in store_fns:
                p.dma("gp", [wbk], [dk],
                      lambda e, sf=sf, wb=wb: sf(e, wb),
                      ["dS", "dSb", "dSc"][s])

        for ci in range(NC):
            sts = []
            for s in range(2):
                sts.append((lambda e, wb, ci=ci, s=s: e.dma_start(
                    out=wqk_t.ap()[s * ET:(s + 1) * ET, ci, :, :].rearrange(
                        "t p j -> p t j"),
                    in_=wb[:, s * ET * 128:(s + 1) * ET * 128].rearrange(
                        "p (t j) -> p t j", j=128)), "wqkT"))
            sts.append((lambda e, wb, ci=ci: e.dma_start(
                out=wv_r.ap()[ci * 128:(ci + 1) * 128, :],
                in_=wb[:, 2 * D:3 * D]), "wvr"))
            prep_iter(lambda e, wf, ci=ci: e.dma_start(
                out=wf[:, 0:3 * D],
                in_=wqkv_d.ap()[ci * 128:(ci + 1) * 128, :]), 3 * D, sts)
        for ci in range(NC):
            sts = [(lambda e, wb, ci=ci, hh=hh: e.dma_start(
                out=wp_h.ap()[2 * ci + hh, :, :],
                in_=wb[hh * 64:(hh + 1) * 64, 0:D]), "wph") for hh in range(2)]
            prep_iter(lambda e, wf, ci=ci: e.dma_start(
                out=wf[:, 0:D],
                in_=wproj_d.ap()[ci * 128:(ci + 1) * 128, :]), D, sts)
        prep_q = []
        TPC = D // 128
        for ci in range(NC):
            for s4 in range(F // D):
                prep_q.append((
                    lambda e, wfv, ci=ci, s4=s4: e.dma_start(
                        out=wfv,
                        in_=w1_d.ap()[ci * 128:(ci + 1) * 128,
                                      s4 * D:(s4 + 1) * D]),
                    lambda e, hb, ci=ci, s4=s4: e.dma_start(
                        out=w1_t.ap()[s4 * TPC:(s4 + 1) * TPC,
                                      ci, :, :].rearrange("t p j -> p t j"),
                        in_=hb[:, 0:D].rearrange("p (t j) -> p t j", j=128)),
                    "w1t"))
        for gch in range(NF):
            prep_q.append((
                lambda e, wfv, gch=gch: e.dma_start(
                    out=wfv, in_=w2_d.ap()[gch * 128:(gch + 1) * 128, 0:D]),
                lambda e, hb, gch=gch: e.dma_start(
                    out=w2_r.ap()[gch * 128:(gch + 1) * 128, :],
                    in_=hb[:, 0:D]),
                "w2r"))
        prep_i = [0]
        wfvs = [wst[i].bitcast(F32)[:, 0:D] for i in range(2)]

        def drain_prep(n):
            for _ in range(n):
                if not prep_q:
                    return
                ldf, stf, dk = prep_q.pop(0)
                s = prep_i[0] % 2
                prep_i[0] += 1
                wk, hk = f"wst{s}", f"hpb{s}"
                p.dma("sy", [wk], [wk],
                      lambda e, ldf=ldf, s=s: ldf(e, wfvs[s]), "dW")
                p.c("ve", [wk], [hk], lambda e, s=s: e.tensor_copy(
                    out=hp_b[s][:, 0:D], in_=wfvs[s]))
                p.dma("gp", [hk], [dk],
                      lambda e, stf=stf, s=s: stf(e, hp_b[s]), "dS")

        p.barrier()

        for r in range(R):
            xrow = x_d.ap()[r * T:(r + 1) * T, :]
            yrow = y_d.ap()[r * T:(r + 1) * T, :]

            p.dma("sy", [], ["wv_all", "zgT", "y2f"], lambda e: e.dma_start(
                out=wv_all.rearrange("p (c d) -> p c d", d=D),
                in_=wv_r.ap().rearrange("(c p) d -> p c d", p=128)), "dWv")
            for h in range(H):
                p.c("ve", [], ["vones"] + (["zgT", "y2f"] if h == 0 else []),
                    lambda e, h=h: e.memset(
                    vsb.rearrange("p (t w) -> p t w", w=VW)[
                        :, :, h * (HD + 1) + HD:h * (HD + 1) + HD + 1], 1.0))

            def emit_vgen(tt, vp, vpk):
                for ci in range(NC):
                    for hv in range(NDH):
                        wl = [vpk] if ci in (0, NC - 1) and hv in (0, NDH - 1) else []
                        if tt <= 1 and ci == 0 and hv == 0:
                            wl = wl + ["PDh0", "PDh1"]
                        p.c("te", ["wv_all", "h1T"], wl,
                            lambda e, ci=ci, tt=tt, hv=hv, vp=vp: e.matmul(
                                out=vp[:, hv * 512:(hv + 1) * 512],
                                lhsT=h1T[:, ci * T + tt * 128:ci * T + (tt + 1) * 128],
                                rhs=wv_all[:, ci * D + hv * 512:
                                           ci * D + hv * 512 + 512],
                                start=(ci == 0), stop=(ci == NC - 1)))
                vdst = vsb[:, tt * VW:(tt + 1) * VW].rearrange(
                    "p (h w) -> p h w", w=HD + 1)[:, :, 0:HD]
                vsrc = vp[:, 0:D].rearrange("p (h w) -> p h w", w=HD)
                ex_v = ["zgT", "y2f"] if tt <= 1 else []
                p.c("se", [vpk, "vones"], ["vsb", vpk] + ex_v,
                    lambda e, vdst=vdst, vsrc=vsrc: e.copy(out=vdst, in_=vsrc))

            for tt in range(NT):
                xb, xk = xin[tt % 2], f"xin{tt % 2}"
                hb, hk = hp_b[tt % 2], f"hpb{tt % 2}"
                p.dma("sy", [xk], [xk], lambda e, tt=tt, xb=xb, xrow=xrow:
                      e.dma_start(out=xb,
                                  in_=xrow[tt * 128:(tt + 1) * 128, :]),
                      "dXa" if tt % 2 == 0 else "dXb")
                sl = tt % 2
                ln_stats([xk], xb, sl)
                p.c("ve", [xk, f"mcol{sl}", f"rcol{sl}"], [hk],
                    lambda e, xb=xb, hb=hb, sl=sl: e.tensor_scalar(
                        out=hb, in0=xb, scalar1=mcol[sl], scalar2=rcol[sl],
                        op0=ALU.subtract, op1=ALU.mult))
                half = (tt % 2) * 1024
                hfk = f"PAh{tt % 2}"
                for ci in range(NC):
                    ex_t = ["PA"] if (tt == 0 and ci == 0) else []
                    p.c("te", [hk, "ident_b"], [hfk] + ex_t,
                        lambda e, hb=hb, ci=ci, half=half: e.transpose(
                            out=PAb[:, half + ci * 128:half + (ci + 1) * 128],
                            in_=hb[:, ci * 128:(ci + 1) * 128],
                            identity=ident_b))
                dst = h1T.rearrange("p (c t) -> p c t", c=NC)[
                    :, :, tt * 128:(tt + 1) * 128]
                srcp = PAb[:, half:half + NC * 128].rearrange(
                    "p (c j) -> p c j", j=128)
                ex_h = (["srep", "selT", "zgT", "y2f", "usb"] if tt == 0 else [])
                p.c("ve", [hfk], ["h1T", hfk] + ex_h,
                    lambda e, dst=dst, srcp=srcp: e.tensor_copy(out=dst, in_=srcp))
                if tt % 2 == 1:
                    emit_vgen(tt - 1, PC, "PC")
                    emit_vgen(tt, PD, "PD")

            for et in range(ET):
                wb, wk = wst[et % 2], f"wst{et % 2}"
                p.dma("sy", [wk], [wk], lambda e, et=et, wb=wb: e.dma_start(
                    out=wb[:, 0:NC * 128].rearrange("p (c j) -> p c j", j=128),
                    in_=wqk_t.ap()[ET + et, :, :, :].rearrange(
                        "c p j -> p c j")), "dWka" if et % 2 == 0 else "dWkb")
                for qc4 in range(T // 512):
                    pk = PK[qc4 % 2]
                    ps = PS[qc4 % 2]
                    for ci in range(NC):
                        wl = [pk] if ci in (0, NC - 1) else []
                        if et == 0 and qc4 == 0 and ci == 0:
                            wl = wl + ["PAh0", "PAh1"]
                        p.c("te", [wk, "h1T"], wl,
                            lambda e, wb=wb, ci=ci, qc4=qc4, ps=ps: e.matmul(
                                out=ps[:, 0:512],
                                lhsT=wb[:, ci * 128:(ci + 1) * 128],
                                rhs=h1T[:, ci * T + qc4 * 512:ci * T + (qc4 + 1) * 512],
                                start=(ci == 0), stop=(ci == NC - 1)))
                    p.c("se", [pk, "kqb1"], ["kqb1", pk],
                        lambda e, ps=ps, qc4=qc4: e.copy(
                            out=kqb[1][:, qc4 * 512:(qc4 + 1) * 512],
                            in_=ps[:, 0:512]))
                p.dma("gp", ["kqb1"], ["kqb1", "kdram"], lambda e, et=et:
                      e.dma_start(out=k_dram.ap()[et, :, 0:T],
                                  in_=kqb[1][:, 0:T]), "dK")
            for et in range(ET):
                wb, wk = wst[et % 2], f"wst{et % 2}"
                p.dma("sy", [wk], [wk], lambda e, et=et, wb=wb: e.dma_start(
                    out=wb[:, 0:NC * 128].rearrange("p (c j) -> p c j", j=128),
                    in_=wqk_t.ap()[et, :, :, :].rearrange("c p j -> p c j")),
                    "dWqa" if et % 2 == 0 else "dWqb")
                for qc in range(NQC):
                    qpk = "PC" if qc % 2 == 0 else "PD"
                    qps = PC if qc % 2 == 0 else PD
                    for ci in range(NC):
                        p.c("te", [wk, "h1T"],
                            [qpk] if ci in (0, NC - 1) else [],
                            lambda e, wb=wb, ci=ci, qc=qc, qps=qps: e.matmul(
                                out=qps[:, 0:512],
                                lhsT=wb[:, ci * 128:(ci + 1) * 128],
                                rhs=h1T[:, ci * T + qc * 512:ci * T + (qc + 1) * 512],
                                start=(ci == 0), stop=(ci == NC - 1)))
                    kb, kk = kqb[0], "kqb0"
                    p.c("se", [qpk, kk + ("w" if True else "")], [kk, qpk],
                        lambda e, kb=kb, qc=qc, qps=qps: e.copy(
                            out=kb[:, qc * 512:(qc + 1) * 512], in_=qps[:, 0:512]))
                p.dma("gp", ["kqb0"], ["kqb0", "qbd"], lambda e, et=et:
                      e.dma_start(
                          out=qb_d.ap()[:, :, et * 512:(et + 1) * 512].rearrange(
                              "q p d -> p q d"),
                          in_=kqb[0].rearrange("p (q d) -> p q d", d=512)), "dQ")

            qsbs = [qst, qst2]

            def emit_qst_load(qc):
                qsb = qsbs[qc % 2]
                p.dma("sy", ["qbd", f"qst{qc % 2}"], [f"qst{qc % 2}"],
                      lambda e, qc=qc, qsb=qsb: e.dma_start(
                          out=qsb, in_=qb_d.ap()[qc, :, :]), "dQ2")

            def emit_kst_load(qc, hp):
                nkt = qc * DKT + DKT
                ks, kk = kst[hp % 2], f"kst{hp % 2}"
                p.dma("sy", ["kdram", kk], [kk], lambda e, hp=hp, nkt=nkt, ks=ks:
                      e.dma_start(out=ks[:, 0:nkt * 128],
                                  in_=k_dram.ap()[hp, :, 0:nkt * 128]), "dK2")

            emit_qst_load(0)
            emit_kst_load(0, 0)

            p.barrier()

            p.dma("sy", [], ["wpj", "h1T"], lambda e: e.dma_start(
                out=wpj[0:64, :].rearrange("r (m d) -> r m d", d=D),
                in_=wp_h.ap().rearrange("(m two) r d -> two r m d", two=2)[0]),
                "dWp")
            p.dma("sy", [], ["wpj", "h1T"], lambda e: e.dma_start(
                out=wpj[64:128, :].rearrange("r (m d) -> r m d", d=D),
                in_=wp_h.ap().rearrange("(m two) r d -> two r m d", two=2)[1]),
                "dWp")

            pend_se = [None]
            pend_rest = [None]

            def flush_se():
                if pend_se[0] is not None:
                    pend_se[0]()
                    pend_se[0] = None

            def flush_rest():
                flush_se()
                if pend_rest[0] is not None:
                    pend_rest[0]()
                    pend_rest[0] = None

            for qc in range(NQC):
                qsb = qsbs[qc % 2]
                qsk = f"qst{qc % 2}"
                nkt = qc * DKT + DKT
                for hp in range(NHP):
                    et = hp
                    ks, kk = kst[hp % 2], f"kst{hp % 2}"
                    if hp > 0:
                        emit_kst_load(qc, hp)
                    if r == 0:
                        drain_prep(2)
                    pvp = PC if hp % 2 == 0 else PD
                    pvk = "PC" if hp % 2 == 0 else "PD"

                    def emit_scores(kt, nkt=nkt, et=et, ks=ks, kk=kk, hp=hp,
                                    qsb=qsb, qsk=qsk):
                        diag = kt >= nkt - DKT
                        spk = PK[kt % 2]
                        sp = PS[kt % 2]
                        for hh in range(2):
                            p.c("te", [kk, qsk], [spk],
                                lambda e, kt=kt, hh=hh, sp=sp, et=et, ks=ks,
                                qsb=qsb: e.matmul(
                                    out=sp[:, hh * 512:(hh + 1) * 512],
                                    lhsT=ks[hh * 64:hh * 64 + 64,
                                            kt * 128:(kt + 1) * 128],
                                    rhs=qsb[hh * 64:hh * 64 + 64,
                                            et * 512:(et + 1) * 512],
                                    start=True, stop=True,
                                    tile_position=(hh * 64, 0)))
                        atb = attb[kt % 6]
                        atk = f"attb{kt % 6}"
                        p.c("se", [spk, "cpool2"], [atk, spk, W8A[kt % 6]],
                            lambda e, sp=sp, atb=atb: e.activation(
                                out=atb, in_=sp, func=AF.Exp, scale=0.125,
                                bias=cpool[:, 1:2]))
                        if diag:
                            rl = kt - (nkt - DKT)
                            for hh in range(2):
                                p.c("ve", [atk, "masks"], [atk],
                                    lambda e, rl=rl, hh=hh, atb=atb: e.tensor_tensor(
                                        out=atb[:, hh * 512:(hh + 1) * 512],
                                        in0=atb[:, hh * 512:(hh + 1) * 512],
                                        in1=masks[:, (DKT - 1 - rl) * 128:
                                                  (DKT - 1 - rl) * 128 + QCW],
                                        op=ALU.mult))

                    def emit_pv(kt, nkt=nkt, hp=hp, pvp=pvp, pvk=pvk):
                        atb = attb[kt % 6]
                        atk = f"attb{kt % 6}"
                        for hh in range(2):
                            vt = vsb[:, kt * VW + (2 * hp + hh) * (HD + 1):
                                     kt * VW + (2 * hp + hh) * (HD + 1) + HD + 1]
                            p.c("te", ["vsb", atk],
                                [pvk] if ((kt == 0 and hh == 0) or
                                          (kt == nkt - 1 and hh == 1)) else [],
                                lambda e, vt=vt, hh=hh, atb=atb,
                                kt=kt, nkt=nkt, pvp=pvp: e.matmul(
                                    out=pvp[0:65, hh * 512:(hh + 1) * 512],
                                    lhsT=vt,
                                    rhs=atb[:, hh * 512:(hh + 1) * 512],
                                    start=(kt == 0), stop=(kt == nkt - 1)))

                    emit_scores(0)
                    emit_scores(1)
                    for ktp in range(2, nkt, 2):
                        emit_scores(ktp)
                        emit_scores(ktp + 1)
                        if ktp >= 4:
                            emit_pv(ktp - 4)
                            emit_pv(ktp - 3)
                        if ktp == 2:
                            flush_se()
                        if ktp == 4:
                            flush_rest()
                    for kt in range(max(0, nkt - 4), nkt):
                        emit_pv(kt)
                    flush_rest()

                    def make_norm(hp=hp, pvp=pvp, pvk=pvk):
                        op = "PA"
                        opp = PA

                        def norm_se():
                            p.c("se", [pvk], ["recb"],
                                lambda e: e.activation(
                                    out=recb[0:1, 0:1024],
                                    in_=pvp[64:65, 0:1024], func=AF.Ln))
                            p.c("se", ["recb"], ["recbb"],
                                lambda e: e.activation(
                                    out=recbb, in_=recb[0:1, 0:1024],
                                    func=AF.Exp, scale=-1.0))

                        def norm_rest():
                            for hh in range(2):
                                p.c("te", ["recbb", "onesb"], [op],
                                    lambda e, hh=hh: e.matmul(
                                        out=opp[0:64, hh * 512:(hh + 1) * 512],
                                        lhsT=onesb[:, 0:64],
                                        rhs=recbb[0:1, hh * 512:(hh + 1) * 512],
                                        start=True, stop=True))
                            p.c("ve", [op], ["repb", op], lambda e: e.tensor_copy(
                                out=repb[0:64, 0:1024], in_=opp[0:64, 0:1024]))
                            p.c("ve", [pvk, "repb"], [f"ao2e{hp}", pvk],
                                lambda e, hp=hp: e.tensor_tensor(
                                    out=ao[0:64, hp * QCW:(hp + 1) * QCW],
                                    in0=pvp[0:64, 0:QCW],
                                    in1=repb[0:64, 0:QCW], op=ALU.mult))
                            p.c("ve", [pvk, "repb"], [f"aostg{hp}", pvk],
                                lambda e, hp=hp: e.tensor_tensor(
                                    out=aostg[0:64, hp * QCW:(hp + 1) * QCW],
                                    in0=pvp[0:64, QCW:2 * QCW],
                                    in1=repb[0:64, QCW:2 * QCW], op=ALU.mult))
                            p.dma("gp", [f"aostg{hp}"], [f"ao2o{hp}"],
                                  lambda e, hp=hp: e.dma_start(
                                      out=ao[64:128, hp * QCW:(hp + 1) * QCW],
                                      in_=aostg[0:64, hp * QCW:(hp + 1) * QCW]),
                                  "dX2")
                        return norm_se, norm_rest

                    pend_se[0], pend_rest[0] = make_norm()
                flush_se()
                if qc + 1 < NQC:
                    emit_qst_load(qc + 1)
                    emit_kst_load(qc + 1, 0)

                for ttl in range(4):
                    tt = qc * 4 + ttl
                    pk = PK[(ttl + 1) % 2]
                    ps = PS[(ttl + 1) % 2]
                    pkO = PK[2 + ttl % 2]
                    psO = PS[2 + ttl % 2]
                    for m in range(H // 2):
                        if ttl == 0 and m == 1:
                            flush_rest()
                        for hv in range(NDH):
                            last = (m in (0, H // 2 - 1) and hv in (0, NDH - 1))
                            p.c("te", [f"ao2e{m}", "wpj"], [pk] if last else [],
                                lambda e, m=m, hv=hv, ttl=ttl, ps=ps: e.matmul(
                                    out=ps[:, hv * 512:(hv + 1) * 512],
                                    lhsT=ao[0:64, m * QCW + ttl * 128:
                                            m * QCW + ttl * 128 + 128],
                                    rhs=wpj[0:64, m * D + hv * 512:
                                            m * D + hv * 512 + 512],
                                    start=(m == 0), stop=(m == H // 2 - 1),
                                    tile_position=(0, 0)))
                            p.c("te", [f"ao2o{m}", "wpj"], [pkO] if last else [],
                                lambda e, m=m, hv=hv, ttl=ttl, psO=psO: e.matmul(
                                    out=psO[:, hv * 512:(hv + 1) * 512],
                                    lhsT=ao[64:128, m * QCW + ttl * 128:
                                            m * QCW + ttl * 128 + 128],
                                    rhs=wpj[64:128, m * D + hv * 512:
                                            m * D + hv * 512 + 512],
                                    start=(m == 0), stop=(m == H // 2 - 1),
                                    tile_position=(64, 0)))
                    xb, xk = xin[tt % 2], f"xin{tt % 2}"
                    p.dma("sy", [xk], [xk], lambda e, tt=tt, xb=xb, xrow=xrow: e.dma_start(
                        out=xb, in_=xrow[tt * 128:(tt + 1) * 128, :]),
                        "dXa" if tt % 2 == 0 else "dXb")
                    sl = ttl % 2
                    tf, tfk = tmpf[sl], f"tmpfa{sl}"
                    tf2, tf2k = tmpf2[sl], f"tmpfb{sl}"
                    p.c("se", [pk], [tfk, pk],
                        lambda e, ps=ps, tf=tf: e.activation(
                            out=tf, in_=ps[:, 0:D], func=AF.Identity))
                    p.c("ve", [pkO, tfk], [tfk, pkO],
                        lambda e, psO=psO, tf=tf: e.tensor_tensor(
                            out=tf, in0=psO[:, 0:D], in1=tf, op=ALU.add))
                    ln_stats([tfk], tf, sl)
                    p.c("ve", [tfk, f"mcol{sl}", f"rcol{sl}"], [tfk],
                        lambda e, sl=sl, tf=tf: e.tensor_scalar(
                            out=tf, in0=tf, scalar1=mcol[sl],
                            scalar2=rcol[sl],
                            op0=ALU.subtract, op1=ALU.mult))
                    p.c("ve", [tfk, xk], [tf2k, pk],
                        lambda e, xb=xb, tf=tf, tf2=tf2: e.tensor_tensor(
                            out=tf2, in0=tf, in1=xb, op=ALU.add))
                    p.dma("gp", [tf2k], [], lambda e, tt=tt, tf2=tf2: e.dma_start(
                        out=x2_d.ap()[tt * 128:(tt + 1) * 128, :], in_=tf2), "dX2")
                    ln_stats([tf2k], tf2, sl)
                    p.c("ve", [tf2k, f"mcol{sl}", f"rcol{sl}"], [tfk],
                        lambda e, sl=sl, tf=tf, tf2=tf2: e.tensor_scalar(
                            out=tf, in0=tf2, scalar1=mcol[sl], scalar2=rcol[sl],
                            op0=ALU.subtract, op1=ALU.mult))
                    p.c("gp", [tfk], ["h2"], lambda e, tt=tt, tf=tf: e.tensor_copy(
                        out=h2[:, tt * D:(tt + 1) * D], in_=tf))
                    p.c("ve", [tfk, "wr_rep"], ["scol", "attb0", W8A[0]],
                        lambda e, tt=tt, tf=tf: e.scalar_tensor_tensor(
                            out=attb[0][:, 0:D], in0=tf, scalar=1.0,
                            in1=wr_rep, op0=ALU.mult, op1=ALU.mult,
                            accum_out=scol[:, tt:tt + 1]))

            if r == 0:
                drain_prep(len(prep_q))
            p.barrier()

            p.c("te", ["scol", "ident_f"], ["PD"], lambda e: e.transpose(
                out=PD[0:NT, 0:128], in_=scol, identity=ident_f))
            p.c("ve", ["PD"], ["st16", "PD"], lambda e: e.tensor_copy(
                out=st16, in_=PD[0:NT, 0:128]))
            p.dma("gp", ["st16"], ["sr_dram"], lambda e: e.dma_start(
                out=sr_d.ap().rearrange("o (t q) -> (o t) q", q=128),
                in_=st16), "dSR")
            p.dma("sy", ["sr_dram"], ["srow", "wpj"], lambda e: e.dma_start(
                out=srow, in_=sr_d.ap()), "dSR2")
            for h in range(T // 512):
                pk = PK[h % 2]
                p.c("te", ["srow", "onesf"], [pk], lambda e, h=h: e.matmul(
                    out=PS[h % 2][:, 0:512], lhsT=onesf,
                    rhs=srow[0:1, h * 512:(h + 1) * 512], start=True, stop=True))
                p.c("ve", [pk], ["srep", pk], lambda e, h=h: e.tensor_copy(
                    out=srep[:, h * 512:(h + 1) * 512], in_=PS[h % 2][:, 0:512]))
            NRC = (T + 1023) // 1024
            for tt in range(NT):
                for c in range(NRC):
                    dst = rankc[:, tt:tt + 1] if c == 0 else tmpc[:, 0:1]
                    dk = "rankc" if c == 0 else "tmpc"
                    pk = PK[c % 2]
                    w = min(1024, T - c * 1024)
                    p.c("ve", ["srep", "scol"], [dk, pk],
                        lambda e, tt=tt, c=c, dst=dst, w=w, pk=pk: e.tensor_scalar(
                            out=PS[PK.index(pk)][:, 0:w],
                            in0=srep[:, c * 1024:c * 1024 + w],
                            scalar1=scol[:, tt:tt + 1],
                            scalar2=0.0, op0=ALU.is_gt, op1=ALU.add,
                            accum_out=dst))
                if NRC > 1:
                    p.c("ve", ["rankc", "tmpc"], ["rankc"],
                        lambda e, tt=tt: e.tensor_tensor(
                            out=rankc[:, tt:tt + 1], in0=rankc[:, tt:tt + 1],
                            in1=tmpc[:, 0:1], op=ALU.add))
            p.c("ve", ["rankc"], ["tmpc"], lambda e: e.tensor_scalar(
                out=tmpc[:, 0:NT], in0=rankc[:, 0:NT], scalar1=float(KSEL),
                scalar2=1.0e6, op0=ALU.is_ge, op1=ALU.mult))
            p.c("ve", ["rankc", "tmpc"], ["sslot"], lambda e: e.tensor_tensor(
                out=sslot[:, 0:NT], in0=rankc[:, 0:NT], in1=tmpc[:, 0:NT],
                op=ALU.add))
            p.c("se", ["scol"], ["wcol"], lambda e: e.activation(
                out=wcol[:, 0:NT], in_=scol[:, 0:NT], func=AF.Tanh, scale=0.5))
            p.c("ve", ["wcol"], ["wcol"], lambda e: e.tensor_scalar(
                out=wcol[:, 0:NT], in0=wcol[:, 0:NT], scalar1=0.5, scalar2=0.5,
                op0=ALU.mult, op1=ALU.add))
            p.c("ve", ["rankc"], ["tmpc"], lambda e: e.tensor_scalar(
                out=tmpc[:, 0:NT], in0=rankc[:, 0:NT], scalar1=float(KSEL),
                scalar2=None, op0=ALU.is_lt))
            p.c("ve", ["wcol", "tmpc"], ["wcol"], lambda e: e.tensor_tensor(
                out=wcol[:, 0:NT], in0=wcol[:, 0:NT], in1=tmpc[:, 0:NT],
                op=ALU.mult))


            cgs = [list(range(gstart, min(gstart + 3, NC)))
                   for gstart in range(0, NC, 3)]
            first_sel = [True]
            for cg in cgs:
                for tt in range(NT):
                    pt, ptk = ptb[tt % 2], f"ptb{tt % 2}"
                    p.c("ve", ["sslot", "iota"], [ptk, W8A[tt % 2]],
                        lambda e, tt=tt, pt=pt: e.tensor_scalar(
                            out=pt, in0=iota[:, 0:KSEL],
                            scalar1=sslot[:, tt:tt + 1], scalar2=None,
                            op0=ALU.is_equal))
                    if cg[0] == 0:
                        for hs in range(KH):
                            p.c("te", [ptk, "iotacb"],
                                ["PD"] if (tt in (0, NT - 1) and hs in (0, KH - 1)) else [],
                                lambda e, tt=tt, hs=hs, pt=pt: e.matmul(
                                    out=PD[0:1, hs * 512:(hs + 1) * 512],
                                    lhsT=iotacb,
                                    rhs=pt[:, hs * 512:(hs + 1) * 512],
                                    start=(tt == 0), stop=False))
                            p.c("te", [ptk, "toffb"], [],
                                lambda e, tt=tt, hs=hs, pt=pt: e.matmul(
                                    out=PD[0:1, hs * 512:(hs + 1) * 512],
                                    lhsT=toffb[:, tt:tt + 1],
                                    rhs=pt[:, hs * 512:(hs + 1) * 512],
                                    start=False, stop=(tt == NT - 1)))
                    for gi, ci in enumerate(cg):
                        pk = PK[gi]
                        ps = PS[gi]
                        for hs in range(KH):
                            p.c("te", ["h2", ptk],
                                [pk] if (tt in (0, NT - 1) and hs in (0, KH - 1)) else [],
                                lambda e, tt=tt, ci=ci, hs=hs, ps=ps, pt=pt: e.matmul(
                                    out=ps[:, hs * 512:(hs + 1) * 512],
                                    lhsT=h2[:, tt * D + ci * 128:
                                            tt * D + (ci + 1) * 128],
                                    rhs=pt[:, hs * 512:(hs + 1) * 512],
                                    start=(tt == 0), stop=(tt == NT - 1)))
                for gi, ci in enumerate(cg):
                    pk = PK[gi]
                    ps = PS[gi]
                    extra = ["srow", "srep"] if first_sel[0] else []
                    first_sel[0] = False
                    p.c("ve", [pk], ["selT", pk] + extra,
                        lambda e, ci=ci, ps=ps: e.tensor_copy(
                            out=selT[:, ci * KSEL:(ci + 1) * KSEL],
                            in_=ps[:, 0:KSEL]))

            p.c("ve", ["PD"], ["tofr", "PD", "srow"], lambda e: e.tensor_copy(
                out=tofr, in_=PD[0:1, 0:KSEL]))
            p.dma("gp", ["tofr"], ["tofdram"], lambda e: e.dma_start(
                out=tof_d.ap().rearrange("(o k) w -> o (k w)", o=1),
                in_=tofr), "dTF")
            for jt in range(NJ):
                p.dma("sy", ["tofdram"], ["tofcf"], lambda e, jt=jt: e.dma_start(
                    out=tofcf[:, jt:jt + 1],
                    in_=tof_d.ap()[jt * 128:(jt + 1) * 128, :]), "dTG")
            p.c("ve", ["tofcf"], ["tofc"], lambda e: e.tensor_copy(
                out=tofc, in_=tofcf))

            for ft in range(NF):
                wb, wk = wst[ft % 2], f"wst{ft % 2}"
                p.dma("sy", ["w1t", wk], [wk], lambda e, ft=ft, wb=wb: e.dma_start(
                    out=wb[:, 0:NC * 128].rearrange("p (c j) -> p c j", j=128),
                    in_=w1_t.ap()[ft, :, :, :].rearrange("c p j -> p c j")),
                    "dW1a" if ft % 2 == 0 else "dW1b")
                pk = PK[ft % 2]
                ps = PS[ft % 2]
                for ci in range(NC):
                    for hs in range(KH):
                        p.c("te", [wk, "selT"],
                            [pk] if (ci in (0, NC - 1) and hs in (0, KH - 1)) else [],
                            lambda e, wb=wb, ci=ci, hs=hs, ps=ps: e.matmul(
                                out=ps[:, hs * 512:(hs + 1) * 512],
                                lhsT=wb[:, ci * 128:(ci + 1) * 128],
                                rhs=selT[:, ci * KSEL + hs * 512:
                                         ci * KSEL + hs * 512 + 512],
                                start=(ci == 0), stop=(ci == NC - 1)))
                p.c("se", [pk, "b1c"], ["zgT", pk],
                    lambda e, ft=ft, ps=ps: e.activation(
                        out=zgT[:, ft * KSEL:(ft + 1) * KSEL],
                        in_=ps[:, 0:KSEL], func=AF.Gelu_apprx_tanh,
                        bias=b1c[:, ft:ft + 1]))

            JH = max(1, NJ // 4)
            for jh in range(JH):
                jts = list(range(jh * NJ // JH, (jh + 1) * NJ // JH))
                for ft in range(NF):
                    wb, wk = wst[ft % 2], f"wst{ft % 2}"
                    p.dma("sy", ["w2r", wk], [wk], lambda e, ft=ft, wb=wb: e.dma_start(
                        out=wb[:, 0:D],
                        in_=w2_r.ap()[ft * 128:(ft + 1) * 128, :]),
                        "dW2a" if ft % 2 == 0 else "dW2b")
                    for ji, jt in enumerate(jts):
                        pk = PK[ji]
                        ps = PS[ji]
                        for hv in range(NDH):
                            p.c("te", [wk, "zgT"],
                                [pk] if (ft == 0 and hv == 0) else [],
                                lambda e, wb=wb, ft=ft, jt=jt, hv=hv, ps=ps: e.matmul(
                                    out=ps[:, hv * 512:(hv + 1) * 512],
                                    lhsT=zgT[:, ft * KSEL + jt * 128:
                                             ft * KSEL + (jt + 1) * 128],
                                    rhs=wb[:, hv * 512:(hv + 1) * 512],
                                    start=(ft == 0), stop=False))
                for ji, jt in enumerate(jts):
                    pk = PK[ji]
                    ps = PS[ji]
                    for hv in range(NDH):
                        p.c("te", ["onesb", "b2row"], [pk],
                            lambda e, hv=hv, ps=ps: e.matmul(
                                out=ps[:, hv * 512:(hv + 1) * 512],
                                lhsT=onesb, rhs=b2row[:, hv * 512:(hv + 1) * 512],
                                start=False, stop=True))
                    extra_u = ["selT"] if (jh == 0 and ji == 0) else []
                    p.c("ve", [pk], ["usb", pk] + extra_u,
                        lambda e, jt=jt, ps=ps: e.tensor_copy(
                            out=usb[:, jt * D:(jt + 1) * D], in_=ps[:, 0:D]))

            for jt in range(NJ):
                p.dma("gp", ["tofc", "usb"], ["updd"],
                      lambda e, jt=jt: e.indirect_dma_start(
                          out=upd_d.ap(),
                          out_offset=IndirectOffsetOnAxis(tofc[:, jt:jt + 1], 0),
                          in_=usb[:, jt * D:(jt + 1) * D], in_offset=None,
                          oob_is_err=False, bounds_check=T - 1), "dUS")

            for tt in range(NT):
                ub, uk = hp_b[tt % 2], f"hpb{tt % 2}"
                p.dma("sy", ["updd", uk], [uk], lambda e, tt=tt, ub=ub:
                      e.dma_start(out=ub[:, 0:D],
                                  in_=upd_d.ap()[tt * 128:(tt + 1) * 128, :]),
                      "dUPa" if tt % 2 == 0 else "dUPb")
                sl = tt % 2
                tf, tfk = tmpf[sl], f"tmpfa{sl}"
                tf2, tf2k = tmpf2[sl], f"tmpfb{sl}"
                p.c("ve", [uk, "wcol", "h2"], [tf2k],
                    lambda e, tt=tt, ub=ub, tf2=tf2: e.scalar_tensor_tensor(
                        out=tf2, in0=ub[:, 0:D],
                        scalar=wcol[:, tt:tt + 1],
                        in1=h2[:, tt * D:(tt + 1) * D],
                        op0=ALU.mult, op1=ALU.add))
                xb, xk = x2in[tt % 2], f"xin{tt % 2}"
                p.dma("sy", [xk], [xk], lambda e, tt=tt, xb=xb: e.dma_start(
                    out=xb, in_=x2_d.ap()[tt * 128:(tt + 1) * 128, :]),
                    "dX2c" if tt % 2 == 0 else "dX2d")
                ln_stats([tf2k], tf2, sl)
                p.c("ve", [tf2k, f"mcol{sl}", f"rcol{sl}"], [tfk],
                    lambda e, sl=sl, tf=tf, tf2=tf2: e.tensor_scalar(
                        out=tf, in0=tf2, scalar1=mcol[sl], scalar2=rcol[sl],
                        op0=ALU.subtract, op1=ALU.mult))
                ob, ok = oute[tt % 2], f"oute{tt % 2}"
                if tt == 0:
                    extra = ["ptb0", "ptb1", "attb0", "attb1", W8A[0], W8A[1]]
                elif tt == 1:
                    extra = ["pjb0", "pjb1", "attb2", "attb3", W8A[2], W8A[3]]
                else:
                    extra = []
                p.c("gp", [tfk, xk], [ok] + extra,
                    lambda e, xb=xb, ob=ob, tf=tf: e.tensor_tensor(
                        out=ob[:, 0:D], in0=tf, in1=xb, op=ALU.add))
                p.dma("gp", [ok], [ok], lambda e, tt=tt, ob=ob, yrow=yrow: e.dma_start(
                    out=yrow[tt * 128:(tt + 1) * 128, :], in_=ob[:, 0:D]), "dO")

        p.barrier()
        p.c("sy", [], [], lambda e: e.nop())

        with nc.Block() as block:
            p.emit(nc, block, es)

    return nc


def kernel(L0_attn_x, L0_attn_w_qkv, L0_attn_w_proj, L0_ffn_w_router,
           L0_ffn_W1, L0_ffn_b1, L0_ffn_W2, L0_ffn_b2):
    cfg = default_cfg()
    T, D, F, R = cfg["T"], cfg["D"], cfg["F"], cfg["R"]
    ncores = 8
    x = np.ascontiguousarray(np.asarray(L0_attn_x, dtype=np.float32))
    wqkv = np.ascontiguousarray(np.asarray(L0_attn_w_qkv, dtype=np.float32))
    wproj = np.ascontiguousarray(np.asarray(L0_attn_w_proj, dtype=np.float32))
    wr = np.ascontiguousarray(
        np.asarray(L0_ffn_w_router, dtype=np.float32)).reshape(1, D)
    W1 = np.ascontiguousarray(np.asarray(L0_ffn_W1, dtype=np.float32))
    b1 = np.ascontiguousarray(np.asarray(L0_ffn_b1, dtype=np.float32)).reshape(1, F)
    W2 = np.ascontiguousarray(np.asarray(L0_ffn_W2, dtype=np.float32))
    b2 = np.ascontiguousarray(np.asarray(L0_ffn_b2, dtype=np.float32)).reshape(1, D)

    nc = build_program(cfg)
    in_maps = []
    for i in range(ncores):
        xs = np.ascontiguousarray(x[i * R:(i + 1) * R].reshape(R * T, D))
        in_maps.append({"x": xs, "w_qkv": wqkv, "w_proj": wproj,
                        "w_router": wr, "W1": W1, "b1": b1,
                        "W2": W2, "b2": b2})
    res = run_bass_kernel_spmd(nc, in_maps, core_ids=list(range(ncores)))
    outs = [res.results[i]["y"].reshape(R, T, D) for i in range(ncores)]
    return np.concatenate(outs, axis=0)
